# version 10
# baseline (speedup 1.0000x reference)
import math
from contextlib import ExitStack

import numpy as np
import concourse.bass as bass
import concourse.mybir as mybir
from concourse.bass_utils import run_bass_kernel_spmd

F32 = mybir.dt.float32
BF16 = mybir.dt.bfloat16
AF = mybir.ActivationFunctionType
ALU = mybir.AluOpType
AX = mybir.AxisListType

P = 128
DM = 1024
DH = 4096
SEQ = 2048
NSEQ = 2
TOK = SEQ * NSEQ
NT = TOK // P
GT = 512
NGRP = TOK // GT
NIC_DM = DM // P
NOC_DH = DH // P
NH_A = 16
HD_A = 64
NH_B = 8
HD_B = 128
RWIN = SEQ + P - 1
NPOS = 2 * SEQ - 1
RFEAT = 192
EPS = 1e-5
NQT = SEQ // P
NKB = SEQ // P

ALL_ENGINES = ("tensor", "vector", "scalar", "gpsimd", "sync")
AUTO_ENGINES = ("vector", "scalar", "gpsimd")


class Builder:
    def __init__(self, nc):
        self.nc = nc
        self.streams = {e: [] for e in ALL_ENGINES}
        self.sem_order = []
        self.counts = {}
        self.sems = {}
        self.watermark = {e: {} for e in ALL_ENGINES}

    def sem(self, name):
        if name not in self.counts:
            self.sem_order.append(name)
            self.counts[name] = 0
        return name

    def _filter(self, engine, wait):
        wl = []
        wm = self.watermark[engine]
        best = {}
        for s, v in wait:
            if v is None or v <= 0:
                continue
            best[s] = max(best.get(s, 0), v)
        for s, v in best.items():
            if wm.get(s, 0) >= v:
                continue
            wl.append((s, v))
            wm[s] = v
        return wl

    def op(self, engine, fn, wait=(), inc=None, auto=True):
        wl = self._filter(engine, wait)
        tok = None
        incs = []
        if engine in AUTO_ENGINES and auto:
            assert inc is None
            a = self.sem("au_" + engine)
            prev = self.counts[a]
            wm = self.watermark[engine]
            if prev > 0 and wm.get(a, 0) < prev:
                wl.append((a, prev))
                wm[a] = prev
            self.counts[a] += 1
            tok = (a, self.counts[a])
            incs = [(a, 1)]
        elif inc is not None:
            s = self.sem(inc)
            self.counts[s] += 1
            tok = (s, self.counts[s])
            incs = [(s, 1)]
        self.streams[engine].append((fn, tuple(wl), tuple(incs)))
        return tok

    def dma(self, queue, out_ap, in_ap, sem, wait=(), **kw):
        s = self.sem(sem)
        prev = self.counts[s]
        wl = self._filter(queue, list(wait) + ([(s, prev)] if prev else []))
        self.counts[s] += 16
        tok = (s, self.counts[s])
        self.streams[queue].append((
            lambda e, o=out_ap, i=in_ap, kw=kw: e.dma_start(o, i, **kw),
            tuple(wl), ((s, 16),)))
        return tok

    def barrier(self):
        s = self.sem("bar")
        base = self.counts[s]
        for e in ALL_ENGINES:
            w = []
            if e in AUTO_ENGINES:
                a = self.sem("au_" + e)
                if self.counts[a] > 0:
                    w.append((a, self.counts[a]))
            self.counts[s] += 1
            self.streams[e].append((lambda eng: eng.nop(), tuple(w),
                                    ((s, 1),)))
        target = base + len(ALL_ENGINES)
        for e in ALL_ENGINES:
            self.streams[e].append((None, ((s, target),), ()))
            self.watermark[e] = {s: target}
        return target

    def wait(self, engine, tokens):
        wl = self._filter(engine, tokens)
        if wl:
            self.streams[engine].append((None, tuple(wl), ()))

    def emit(self, stack):
        nc = self.nc
        for name in self.sem_order:
            self.sems[name] = stack.enter_context(nc.semaphore(name))
        block = stack.enter_context(nc.Block())

        def make(engine_name):
            def _(eng):
                for fn, wait, inc in self.streams[engine_name]:
                    for s, v in wait:
                        eng.wait_ge(self.sems[s], v)
                    if fn is None:
                        continue
                    inst = fn(eng)
                    for s, a in inc:
                        inst.then_inc(self.sems[s], a)
            return _

        block.tensor(make("tensor"))
        block.vector(make("vector"))
        block.scalar(make("scalar"))
        block.gpsimd(make("gpsimd"))
        block.sync(make("sync"))


class Ring:

    def __init__(self, b, name, bufs, queue="sync"):
        self.b = b
        self.bufs = bufs
        self.n = len(bufs)
        self.queue = queue
        self.semn = [b.sem(f"{name}{i}") for i in range(self.n)]
        self.free_gate = [None] * self.n
        self.u = 0

    def load(self, in_ap, extra_wait=()):
        slot = self.u % self.n
        w = list(extra_wait)
        if self.free_gate[slot] is not None:
            w.append(self.free_gate[slot])
        tok = self.b.dma(self.queue, self.bufs[slot][:], in_ap,
                         self.semn[slot], wait=w)
        self.u += 1
        return slot, tok

    def set_free(self, slot, gate):
        self.free_gate[slot] = gate


def _positions_T():
    from scipy.special import gammaln

    seq_len, feature_size = SEQ, RFEAT
    distances = np.arange(-seq_len + 1, seq_len, dtype=np.float64)
    n = feature_size // 6
    absd = np.abs(distances)[:, None]
    max_range = math.log(seq_len) / math.log(2.0)
    half_life = 2.0 ** np.linspace(3.0, max_range, n)
    feat_exp = np.exp(-math.log(2.0) / half_life[None, :] * absd)
    center_widths = 2.0 ** np.arange(1, n + 1, dtype=np.float64) - 1.0
    feat_cm = (center_widths[None, :] > absd).astype(np.float64)
    stddev = seq_len / (2.0 * n)
    start_mean = seq_len / n
    mean = np.linspace(start_mean, float(seq_len), n)[None, :]
    conc = (mean / stddev) ** 2
    rate = mean / stddev**2
    with np.errstate(divide="ignore", invalid="ignore"):
        xlogy = np.where(conc - 1.0 == 0.0, 0.0, (conc - 1.0) * np.log(absd))
    log_unnorm = xlogy - rate * absd
    log_norm = gammaln(conc) - conc * np.log(rate)
    probs = np.exp(log_unnorm - log_norm) + 1e-08
    feat_gamma = probs / np.max(probs, axis=-1, keepdims=True)
    emb = np.concatenate([feat_exp, feat_cm, feat_gamma], axis=-1)
    emb = np.concatenate([emb, np.sign(distances)[:, None] * emb], axis=-1)
    return np.ascontiguousarray(emb.T.astype(np.float32))


class Prog:
    def __init__(self, phases=("attn_a", "ffn1", "attn_b", "ffn2")):
        self.phases = phases
        self.tap = None
        self.tap_i = 0
        self._tap_hit = False
        nc = self.nc = bass.Bass("TRN2", target_bir_lowering=False)

        def din(name, shape):
            return nc.dram_tensor(name, shape, F32, kind="ExternalInput")

        self.x_in = din("x", [TOK, DM])
        self.waq = din("waqT", [DM, DM])
        self.wak = din("wakT", [DM, DM])
        self.wav = din("wavT", [DM, DM])
        self.wao = din("waoT", [DM, DM])
        self.wbq = din("wbq", [DM, DM])
        self.wbk = din("wbk", [DM, DM])
        self.wbv = din("wbv", [DM, DM])
        self.wbo = din("wbo", [DM, DM])
        self.wbrel = din("wbrel", [RFEAT, DM])
        self.wbreb = din("wbreb", [NH_B, HD_B])
        self.wbrpb = din("wbrpb", [NH_B, HD_B])
        self.boutrow = din("boutrow", [1, DM])
        self.posT = din("posT", [RFEAT, NPOS])
        self.w1T = din("w1T", [DM, DH])
        self.w2T = din("w2T", [DM, DH])
        self.h1T = din("h1T", [DH, DH])
        self.h2T = din("h2T", [DH, DH])
        self.w3T = din("w3T", [DH, DM])
        self.s1l = din("s1l", [P, NOC_DH])
        self.s2l = din("s2l", [P, NOC_DH])
        self.hs1l = din("hs1l", [P, NOC_DH])
        self.hs2l = din("hs2l", [P, NOC_DH])
        self.s3row = din("s3row", [1, DM])

        self.y_out = nc.dram_tensor("y", [TOK, DM], F32, kind="ExternalOutput")
        self.xs = nc.dram_tensor("xs", [TOK, DM], F32, kind="Internal")
        self.c_w1 = nc.dram_tensor("c_w1", [NOC_DH * P * DM], BF16, kind="Internal")
        self.c_w2 = nc.dram_tensor("c_w2", [NOC_DH * P * DM], BF16, kind="Internal")
        self.c_h1 = nc.dram_tensor("c_h1", [NOC_DH * P * DH], BF16, kind="Internal")
        self.c_h2 = nc.dram_tensor("c_h2", [NOC_DH * P * DH], BF16, kind="Internal")
        self.c_w3 = nc.dram_tensor("c_w3", [DH * DM], BF16, kind="Internal")
        self.c_relkT = nc.dram_tensor("c_relkT", [NH_B * P * NPOS], BF16,
                                      kind="Internal")
        self.NMRING = 6
        self.m_ring = nc.dram_tensor("m_ring", [self.NMRING * P * RWIN], BF16,
                                     kind="Internal")
        self.b = Builder(nc)

    def build(self):
        b, nc = self.b, self.nc
        with ExitStack() as stack:
            def sbuf(name, shape, dtype):
                return stack.enter_context(nc.sbuf_tensor(name, shape, dtype))

            self.identb = sbuf("identb", [P, P], BF16)
            self.s1s = sbuf("s1s", [P, NOC_DH], F32)
            self.s2s = sbuf("s2s", [P, NOC_DH], F32)
            self.hs1s = sbuf("hs1s", [P, NOC_DH], F32)
            self.hs2s = sbuf("hs2s", [P, NOC_DH], F32)
            self.boutbc = sbuf("boutbc", [P, DM], F32)
            self.constv = sbuf("constv", [P, 4], F32)
            self.rebs = sbuf("rebs", [P, NH_B], F32)
            self.rpbs = sbuf("rpbs", [P, NH_B], F32)
            self.dbgbuf = sbuf("dbgbuf", [P, 40], F32)

            self.emit_init(stack)
            b.barrier()

            if not self._tap_hit:
                if "attn_a" in self.phases:
                    self.emit_attn(stack, variant="a", src=self.x_in,
                                   dst=self.xs)
                else:
                    self.emit_copy(self.x_in, self.xs)
                b.barrier()

            if not self._tap_hit and "ffn1" in self.phases:
                self.emit_ffn(stack, src=self.xs, dst=self.xs, tag="f1")
                b.barrier()

            if not self._tap_hit and "attn_b" in self.phases:
                self.emit_attn(stack, variant="b", src=self.xs, dst=self.xs)
                b.barrier()

            if not self._tap_hit:
                if "ffn2" in self.phases:
                    self.emit_ffn(stack, src=self.xs, dst=self.y_out, tag="f2")
                else:
                    self.emit_copy(self.xs, self.y_out)
            b.barrier()
            if getattr(self, "dbg_snap", None) is not None:
                b.dma("gpsimd", self.y_out.ap()[0:P, 0:40],
                      self.dbgbuf[:], "dbgd",
                      wait=[self._dbg1, self._dbg2])
                b.wait("gpsimd", [("dbgd", b.counts["dbgd"])])
            b.emit(stack)
        return nc

    def _tapd(self, buf, width, wait, row0=0):
        b = self.b
        self._tap_hit = True
        last = None
        nrt = max(1, width // DM)
        for t in range(nrt):
            w = min(DM, width)
            last = b.dma("gpsimd",
                         self.y_out.ap()[(row0 + t) * P:(row0 + t + 1) * P, 0:w],
                         buf[:, t * DM:t * DM + w], "tapd", wait=wait)
        b.wait("gpsimd", [last])

    def emit_copy(self, src, dst):
        b = self.b
        last = None
        for t in range(NT):
            last = b.dma("gpsimd",
                         dst.ap()[t * P:(t + 1) * P, :],
                         src.ap()[t * P:(t + 1) * P, :], "cp")
        b.wait("gpsimd", [last])

    def emit_init(self, stack):
        b, nc = self.b, self.nc

        b.op("vector", lambda e: e.memset(self.constv[:, 0:1], EPS))
        b.op("vector", lambda e: e.memset(self.constv[:, 1:2], 0.125))
        b.op("vector", lambda e: e.memset(self.constv[:, 2:3],
                                          float(HD_B) ** -0.5))
        b.op("gpsimd", lambda e: e.memset(self.identb[:], 1.0))
        b.op("gpsimd", lambda e: e.affine_select(
            self.identb[:], self.identb[:], [[1, P]], ALU.is_equal, 0.0,
            base=0, channel_multiplier=-1))

        for src, buf in ((self.s1l, self.s1s), (self.s2l, self.s2s),
                         (self.hs1l, self.hs1s), (self.hs2l, self.hs2s)):
            b.dma("gpsimd", buf[:], src.ap(), "ig")
        for h in range(NH_B):
            b.dma("gpsimd", self.rebs[:, h:h + 1],
                  self.wbreb.ap()[h:h + 1, :], "ig")
            b.dma("gpsimd", self.rpbs[:, h:h + 1],
                  self.wbrpb.ap()[h:h + 1, :], "ig")

        with nc.sbuf_tensor("i_ones", [1, P], BF16) as ones_t, \
             nc.sbuf_tensor("i_brow", [1, DM], BF16) as brow, \
             nc.sbuf_tensor("i_s3r", [1, DM], F32) as s3r, \
             nc.sbuf_tensor("i_s3rb", [1, DM], BF16) as s3rb, \
             nc.sbuf_tensor("i_s3b", [P, DM], F32) as s3bc, \
             nc.sbuf_tensor("i_stgf0", [P, DM], F32) as stgf0, \
             nc.sbuf_tensor("i_stgf1", [P, DM], F32) as stgf1, \
             nc.sbuf_tensor("i_stgb0", [P, DH], BF16) as stgb0, \
             nc.sbuf_tensor("i_stgb1", [P, DH], BF16) as stgb1, \
             nc.sbuf_tensor("i_w3b0", [P, DM], BF16) as w3b0, \
             nc.sbuf_tensor("i_w3b1", [P, DM], BF16) as w3b1, \
             nc.psum_tensor("i_ps", [P, DM], F32) as ips:
            b.op("vector", lambda e: e.memset(ones_t[:], 1.0))
            tb = b.dma("gpsimd", brow[:], self.boutrow.ap(), "ibrow")
            ts = b.dma("gpsimd", s3r[:], self.s3row.ap(), "is3r")
            d0 = b.op("vector", lambda e: e.tensor_copy(s3rb[:], s3r[:]),
                      wait=[ts])
            b.op("tensor", lambda e: e.matmul(ips[:, 0:512], ones_t[:],
                                              brow[:, 0:512], start=True,
                                              stop=True),
                 wait=[tb, d0])
            mk = b.op("tensor", lambda e: e.matmul(ips[:, 512:1024], ones_t[:],
                                                   brow[:, 512:1024],
                                                   start=True, stop=True),
                      inc="ipe")
            d1 = b.op("vector", lambda e: e.tensor_copy(self.boutbc[:], ips[:]),
                      wait=[mk])
            b.op("tensor", lambda e: e.matmul(ips[:, 0:512], ones_t[:],
                                              s3rb[:, 0:512], start=True,
                                              stop=True), wait=[d1])
            mk = b.op("tensor", lambda e: e.matmul(ips[:, 512:1024], ones_t[:],
                                                   s3rb[:, 512:1024],
                                                   start=True, stop=True),
                      inc="ipe")
            s3_ready = b.op("vector", lambda e: e.tensor_copy(s3bc[:], ips[:]),
                            wait=[mk])

            stg = [stgb0, stgb1]
            stg_sem = ["istg0", "istg1"]
            cw_sem = ["icw0", "icw1"]
            stg_free = [None, None]
            u = 0
            for src_t, dst_t, I, O in ((self.w1T, self.c_w1, DM, DH),
                                       (self.w2T, self.c_w2, DM, DH),
                                       (self.h1T, self.c_h1, DH, DH),
                                       (self.h2T, self.c_h2, DH, DH)):
                noc = O // P
                for ic in range(I // P):
                    sl = u % 2
                    w = [stg_free[sl]] if stg_free[sl] else []
                    t = b.dma("gpsimd", stg[sl][:, 0:O],
                              src_t.ap()[ic * P:(ic + 1) * P, :],
                              stg_sem[sl], wait=w)
                    cwt = b.dma("gpsimd",
                                bass.AP(dst_t, ic * P,
                                        [[I, P], [P * I, noc], [1, P]]),
                                stg[sl][:, 0:O].rearrange(
                                    "p (oc o) -> p oc o", oc=noc),
                                cw_sem[sl], wait=[t])
                    stg_free[sl] = cwt
                    u += 1

            stf = [stgf0, stgf1]
            stf_sem = ["istf0", "istf1"]
            cw3_sem = ["icw30", "icw31"]
            stf_free = [None, None]
            w3o = [w3b0, w3b1]
            w3o_free = [None, None]
            for ic in range(DH // P):
                sl = ic % 2
                w = [stf_free[sl]] if stf_free[sl] else []
                t = b.dma("gpsimd", stf[sl][:],
                          self.w3T.ap()[ic * P:(ic + 1) * P, :],
                          stf_sem[sl], wait=w)
                dw = [t, s3_ready]
                if w3o_free[sl]:
                    dw.append(w3o_free[sl])
                d = b.op("vector",
                         lambda e, sl=sl: e.tensor_mul(w3o[sl][:], stf[sl][:],
                                                       s3bc[:]),
                         wait=dw)
                stf_free[sl] = d
                cwt = b.dma("gpsimd",
                            bass.AP(self.c_w3, ic * P * DM, [[DM, P], [1, DM]]),
                            w3o[sl][:], cw3_sem[sl], wait=[d])
                w3o_free[sl] = cwt

            totals = [(s, b.counts[s]) for s in
                      ("ig", "ibrow", "is3r", "istg0", "istg1", "istf0",
                       "istf1", "icw0", "icw1", "icw30", "icw31")
                      if s in b.counts]
            b.wait("gpsimd", totals)

    def emit_ffn(self, stack, src, dst, tag):
        b, nc = self.b, self.nc
        identb = self.identb
        s1s, s2s, hs1s, hs2s = self.s1s, self.s2s, self.hs1s, self.hs2s

        pe = "fpe"
        f_w1 = "ffw1"
        f_w2 = "ffw2"
        f_h = "ffh"
        f_w3 = "ffw3"

        with ExitStack() as st:
            def sbuf(name, shape, dtype):
                return st.enter_context(nc.sbuf_tensor(tag + name, shape, dtype))

            xg = [sbuf("xg0", [P, 4 * DM], F32), sbuf("xg1", [P, 4 * DM], F32)]
            outg = sbuf("outg", [P, 4 * DM], F32)
            usq = sbuf("usq", [P, DM], BF16)
            ub = [sbuf("ub0", [P, DM], BF16), sbuf("ub1", [P, DM], BF16)]
            u_fm = sbuf("ufm", [P, NIC_DM * GT], BF16)
            y1 = sbuf("y1", [P, NOC_DH * GT], BF16)
            y2 = sbuf("y2", [P, NOC_DH * GT], BF16)
            t1b = [sbuf("t1b0", [P, GT], BF16), sbuf("t1b1", [P, GT], BF16)]
            t2b = [sbuf("t2b0", [P, GT], BF16), sbuf("t2b1", [P, GT], BF16)]
            stats = sbuf("stats", [P, 32], F32)

            x_sem = ["fxl0", "fxl1"]
            r_w1 = Ring(b, "frw1",
                        [sbuf("w1b0", [P, DM], BF16), sbuf("w1b1", [P, DM], BF16)])
            r_w2 = Ring(b, "frw2",
                        [sbuf("w2b0", [P, DM], BF16), sbuf("w2b1", [P, DM], BF16)])
            r_h = Ring(b, "frh",
                       [sbuf("hb0", [P, DH], BF16), sbuf("hb1", [P, DH], BF16),
                        sbuf("hb2", [P, DH], BF16)])
            r_w3 = Ring(b, "frw3",
                        [sbuf("w3b0", [P, DM], BF16), sbuf("w3b1", [P, DM], BF16)])

            pW = [st.enter_context(nc.psum_tensor(tag + "pW0", [P, DM], F32)),
                  st.enter_context(nc.psum_tensor(tag + "pW1", [P, DM], F32))]
            pT = [st.enter_context(nc.psum_tensor(tag + "pT0", [P, DM], F32)),
                  st.enter_context(nc.psum_tensor(tag + "pT1", [P, DM], F32))]

            res_by_grp = {}
            out_tok = {}
            last_w2_read_ufm = None
            last_w3_read_y3 = None
            t1b_free = [None, None]
            ufm_evac = {}
            tp_done = {}
            res_last = None

            for grp in range(NGRP):
                slot = grp % 2
                rows = src.ap()[grp * GT:(grp + 1) * GT, :]
                rows_dst = dst.ap()[grp * GT:(grp + 1) * GT, :]
                w = []
                if grp >= 2:
                    w.append(res_by_grp[grp - 2])
                xld = b.dma("gpsimd",
                            xg[slot][:].rearrange("p (t d) -> p t d", t=4),
                            rows.rearrange("(t p) d -> p t d", p=P),
                            x_sem[slot], wait=w)

                for t in range(4):
                    xt = xg[slot][:, t * DM:(t + 1) * DM]
                    c0 = 8 * t
                    s_sum = stats[:, c0:c0 + 1]
                    s_sq = stats[:, c0 + 1:c0 + 2]
                    s_m = stats[:, c0 + 2:c0 + 3]
                    s_v = stats[:, c0 + 3:c0 + 4]
                    s_r0 = stats[:, c0 + 4:c0 + 5]
                    s_r = stats[:, c0 + 5:c0 + 6]
                    s_m2 = stats[:, c0 + 6:c0 + 7]
                    s_nb = stats[:, c0 + 7:c0 + 8]
                    aw = [xld]
                    a1 = b.op("scalar",
                              lambda e, xt=xt, s_sq=s_sq: e.activation(
                                  usq[:], xt, AF.Square, accum_out=s_sq),
                              wait=aw)
                    b.op("vector",
                         lambda e, xt=xt, s_sum=s_sum: e.reduce_sum(
                             s_sum, xt, AX.X), wait=aw)
                    b.op("vector", lambda e, s_m=s_m, s_sum=s_sum:
                         e.tensor_scalar_mul(s_m, s_sum, 1.0 / DM))
                    b.op("vector", lambda e, s_v=s_v, s_sq=s_sq:
                         e.tensor_scalar_mul(s_v, s_sq, 1.0 / DM),
                         wait=[a1])
                    b.op("vector", lambda e, s_m2=s_m2, s_m=s_m:
                         e.tensor_mul(s_m2, s_m, s_m))
                    d2 = b.op("vector", lambda e, s_v=s_v, s_m2=s_m2:
                              e.tensor_sub(s_v, s_v, s_m2))
                    a2 = b.op("scalar", lambda e, s_v=s_v, s_r0=s_r0:
                              e.activation(s_r0, s_v, AF.Sqrt,
                                           bias=self.constv[:, 0:1]),
                              wait=[d2])
                    b.op("vector", lambda e, s_r=s_r, s_r0=s_r0:
                         e.reciprocal(s_r, s_r0), wait=[a2])
                    d4 = b.op("vector", lambda e, s_nb=s_nb, s_m=s_m, s_r=s_r:
                              e.tensor_scalar(s_nb, s_m, s_r, -1.0, ALU.mult,
                                              ALU.mult))
                    uw = [d4]
                    key = (grp, t - 2) if t >= 2 else (grp - 1, t + 2)
                    if key in tp_done:
                        uw.append(tp_done[key])
                    du = b.op("vector",
                              lambda e, xt=xt, t=t, s_r=s_r, s_nb=s_nb:
                              e.tensor_scalar(ub[t % 2][:], xt, s_r, s_nb,
                                              ALU.mult, ALU.add),
                              wait=uw)
                    tw = [du]
                    if key in ufm_evac:
                        tw.append(ufm_evac[key])
                    tp = None
                    for ic in range(NIC_DM):
                        tp = b.op("tensor",
                                  lambda e, t=t, ic=ic: e.matmul(
                                      pT[t % 2][:, ic * P:(ic + 1) * P],
                                      ub[t % 2][:, ic * P:(ic + 1) * P],
                                      identb[:], start=True, stop=True),
                                  wait=tw if ic == 0 else [],
                                  inc=pe if ic == NIC_DM - 1 else None)
                    tp_done[(grp, t)] = tp
                    ew = [tp]
                    if grp > 0 and t == 0 and last_w2_read_ufm:
                        ew.append(last_w2_read_ufm)
                    ufm_out = u_fm[:].rearrange(
                        "p (ic g) -> p ic g", ic=NIC_DM)[:, :, t * P:(t + 1) * P]
                    pt_src = pT[t % 2][:].rearrange(
                        "p (ic q) -> p ic q", ic=NIC_DM)
                    ufm_evac[(grp, t)] = b.op(
                        "vector",
                        lambda e, ufm_out=ufm_out, pt_src=pt_src:
                        e.tensor_copy(ufm_out, pt_src),
                        wait=ew)
                ufm_ready = ufm_evac[(grp, 3)]
                if self.tap == ("ufm", grp):
                    self._tapd(u_fm, NIC_DM * GT, [ufm_ready])
                    return

                y1_mult = {}
                silu_evac = {}
                for oc in range(NOC_DH):
                    sl1, w1g = r_w1.load(
                        bass.AP(self.c_w1, oc * P * DM, [[DM, P], [1, DM]]))
                    sl2, w2g = r_w2.load(
                        bass.AP(self.c_w2, oc * P * DM, [[DM, P], [1, DM]]))
                    psA = pW[oc % 2][:, 0:GT]
                    psB = pW[oc % 2][:, GT:2 * GT]
                    mw = [w1g, ufm_ready]
                    if oc >= 2:
                        mw.append(silu_evac[oc - 2][0])
                    elif res_last:
                        mw.append(res_last)
                    w1_tok = None
                    for ic in range(NIC_DM):
                        w1_tok = b.op(
                            "tensor",
                            lambda e, ic=ic, sl1=sl1, psA=psA: e.matmul(
                                psA, r_w1.bufs[sl1][:, ic * P:(ic + 1) * P],
                                u_fm[:, ic * GT:(ic + 1) * GT],
                                start=(ic == 0), stop=(ic == NIC_DM - 1)),
                            wait=mw if ic == 0 else [],
                            inc=f_w1 if ic == NIC_DM - 1 else None)
                    r_w1.set_free(sl1, w1_tok)
                    mw = [w2g]
                    if oc >= 2:
                        mw.append(silu_evac[oc - 2][1])
                    w2_tok = None
                    for ic in range(NIC_DM):
                        w2_tok = b.op(
                            "tensor",
                            lambda e, ic=ic, sl2=sl2, psB=psB: e.matmul(
                                psB, r_w2.bufs[sl2][:, ic * P:(ic + 1) * P],
                                u_fm[:, ic * GT:(ic + 1) * GT],
                                start=(ic == 0), stop=(ic == NIC_DM - 1)),
                            wait=mw if ic == 0 else [],
                            inc=f_w2 if ic == NIC_DM - 1 else None)
                    r_w2.set_free(sl2, w2_tok)
                    last_w2_read_ufm = w2_tok
                    ew = [w1_tok]
                    if oc >= 2:
                        ew.append(y1_mult[oc - 2])
                    elif t1b_free[oc % 2]:
                        ew.append(t1b_free[oc % 2])
                    e1 = b.op("scalar",
                              lambda e, oc=oc, psA=psA: e.activation(
                                  t1b[oc % 2][:], psA, AF.Silu,
                                  scale=s1s[:, oc:oc + 1]),
                              wait=ew)
                    e2 = b.op("scalar",
                              lambda e, oc=oc, psB=psB: e.activation(
                                  t2b[oc % 2][:], psB, AF.Copy,
                                  scale=s2s[:, oc:oc + 1]),
                              wait=[w2_tok])
                    silu_evac[oc] = (e1, e2)
                    mw = [e2]
                    if oc == 0 and last_w3_read_y3:
                        mw.append(last_w3_read_y3)
                    y1_mult[oc] = b.op(
                        "vector",
                        lambda e, oc=oc: e.tensor_mul(
                            y1[:, oc * GT:(oc + 1) * GT], t1b[oc % 2][:],
                            t2b[oc % 2][:]),
                        wait=mw)
                t1b_free = [y1_mult[NOC_DH - 2], y1_mult[NOC_DH - 1]]
                y1_done = y1_mult[NOC_DH - 1]
                if self.tap == ("y1", grp):
                    self._tapd(y1, NOC_DH * GT, [y1_done])
                    return

                prev_done = y1_done
                h_in = [y1, y2]
                caches = [self.c_h1, self.c_h2]
                scales = [hs1s, hs2s]
                outs = [y2, y1]
                for stage in range(2):
                    gelu_evac = {}
                    for oc in range(NOC_DH):
                        slh, hg = r_h.load(
                            bass.AP(caches[stage], oc * P * DH,
                                    [[DH, P], [1, DH]]))
                        ps = pW[oc % 2][:, 0:GT]
                        mw = [hg, prev_done]
                        if oc >= 2:
                            mw.append(gelu_evac[oc - 2])
                        elif stage == 0:
                            mw.append(silu_evac[NOC_DH - 2 + oc][0])
                        h_tok = None
                        for ic in range(NOC_DH):
                            h_tok = b.op(
                                "tensor",
                                lambda e, ic=ic, slh=slh, ps=ps, stage=stage:
                                e.matmul(
                                    ps, r_h.bufs[slh][:, ic * P:(ic + 1) * P],
                                    h_in[stage][:, ic * GT:(ic + 1) * GT],
                                    start=(ic == 0), stop=(ic == NOC_DH - 1)),
                                wait=mw if ic == 0 else [],
                                inc=f_h if ic == NOC_DH - 1 else None)
                        r_h.set_free(slh, h_tok)
                        gelu_evac[oc] = b.op(
                            "scalar",
                            lambda e, oc=oc, ps=ps, stage=stage: e.activation(
                                outs[stage][:, oc * GT:(oc + 1) * GT], ps,
                                AF.Gelu, scale=scales[stage][:, oc:oc + 1]),
                            wait=[h_tok])
                    prev_done = gelu_evac[NOC_DH - 1]
                    if self.tap == ("y2" if stage == 0 else "y3", grp):
                        self._tapd(outs[stage], NOC_DH * GT, [prev_done])
                        return
                y3_done = prev_done

                res_adds = []
                for hp in range(2):
                    w3_tok = None
                    for ic in range(NOC_DH):
                        sl3, w3g = r_w3.load(
                            bass.AP(self.c_w3, ic * P * DM, [[DM, P], [1, DM]]))
                        mw = [w3g]
                        if ic == 0:
                            mw.append(y3_done)
                            if hp == 1:
                                mw.append(res_adds[-1])
                        for t_loc in range(2):
                            t = hp * 2 + t_loc
                            for half in range(2):
                                ps = pW[t_loc][:, half * GT:(half + 1) * GT]
                                last_mm = (t_loc == 1 and half == 1)
                                w3_tok2 = b.op(
                                    "tensor",
                                    lambda e, ic=ic, sl3=sl3, t=t, ps=ps,
                                    half=half: e.matmul(
                                        ps,
                                        y1[:, ic * GT + t * P:
                                           ic * GT + (t + 1) * P],
                                        r_w3.bufs[sl3][:,
                                                       half * GT:
                                                       (half + 1) * GT],
                                        start=(ic == 0),
                                        stop=(ic == NOC_DH - 1)),
                                    wait=mw if (t_loc == 0 and half == 0)
                                    else [],
                                    inc=f_w3 if last_mm else None)
                                if last_mm:
                                    w3_tok = w3_tok2
                        r_w3.set_free(sl3, w3_tok)
                    w3_done = w3_tok
                    last_w3_read_y3 = w3_done
                    for t_loc in range(2):
                        t = hp * 2 + t_loc
                        rw = [w3_done]
                        if grp > 0 and hp == 0 and t_loc == 0:
                            rw.append(out_tok[grp - 1])
                        r = b.op("vector",
                                 lambda e, t=t, t_loc=t_loc, slot=slot:
                                 e.tensor_add(
                                     outg[:, t * DM:(t + 1) * DM],
                                     pW[t_loc][:],
                                     xg[slot][:, t * DM:(t + 1) * DM]),
                                 wait=rw)
                        res_adds.append(r)
                res_last = res_adds[-1]
                res_by_grp[grp] = res_last
                out_tok[grp] = b.dma("gpsimd",
                                     rows_dst.rearrange("(t p) d -> p t d", p=P),
                                     outg[:].rearrange("p (t d) -> p t d", t=4),
                                     "fxo", wait=[res_last])

            b.wait("gpsimd", [out_tok[NGRP - 1]])

    def emit_attn(self, stack, variant, src, dst):
        b, nc = self.b, self.nc
        is_a = variant == "a"
        tg = "a" if is_a else "t"

        with ExitStack() as ph:
            def sbuf(name, shape, dtype):
                return ph.enter_context(nc.sbuf_tensor(tg + name, shape, dtype))

            qT = sbuf("qT", [P, 8 * SEQ], BF16)
            kT = sbuf("kT", [P, 8 * SEQ], BF16)
            vsb = sbuf("v", [P, NQT * DM], BF16)

            relk_ready = None
            if not is_a:
                with nc.sbuf_tensor(tg + "posT", [P, 2 * NPOS], BF16) as posb, \
                     nc.sbuf_tensor(tg + "wrel", [P, 2 * DM], BF16) as wrelb, \
                     nc.sbuf_tensor(tg + "rko", [P, NPOS], BF16) as rko, \
                     nc.psum_tensor(tg + "rkp", [P, 512], F32) as rkp:
                    t1 = b.dma("gpsimd", posb[:, 0:NPOS],
                               self.posT.ap()[0:P, :], "blpos")
                    t2 = b.dma("gpsimd", posb[0:64, NPOS:2 * NPOS],
                               self.posT.ap()[P:RFEAT, :], "blpos2")
                    t3 = b.dma("gpsimd", wrelb[:, 0:DM],
                               self.wbrel.ap()[0:P, :], "blrel")
                    t3b = b.dma("gpsimd", wrelb[0:64, DM:2 * DM],
                                self.wbrel.ap()[P:RFEAT, :], "blrel2")
                    rk_dma = None
                    rk_prev = None
                    for h in range(NH_B):
                        ev_prev = None
                        for c in range(8):
                            n = 512 if c < 7 else 511
                            mm = None
                            for kc in range(2):
                                rows = P if kc == 0 else 64
                                mm = b.op(
                                    "tensor",
                                    lambda e, h=h, c=c, kc=kc, rows=rows, n=n:
                                    e.matmul(
                                        rkp[:, 0:n],
                                        wrelb[0:rows,
                                              kc * DM + h * P:
                                              kc * DM + (h + 1) * P],
                                        posb[0:rows,
                                             kc * NPOS + c * 512:
                                             kc * NPOS + c * 512 + n],
                                        start=(kc == 0), stop=(kc == 1)),
                                    wait=([t for t in
                                           (t1, t2, t3, t3b, ev_prev, rk_prev)
                                           if t] if kc == 0 else []),
                                    inc="brpe" if kc == 1 else None)
                            ev_prev = b.op(
                                "vector",
                                lambda e, c=c, n=n: e.tensor_copy(
                                    rko[:, c * 512:c * 512 + n], rkp[:, 0:n]),
                                wait=[mm, rk_prev] if rk_prev else [mm])
                        rk_dma = b.dma(
                            "gpsimd",
                            bass.AP(self.c_relkT, h * P * NPOS,
                                    [[NPOS, P], [1, NPOS]]),
                            rko[:], "blrk", wait=[ev_prev])
                        rk_prev = rk_dma
                    relk_ready = rk_dma
                    b.wait("gpsimd", [("blrk", b.counts["blrk"])])

            for s in range(NSEQ):
                self._attn_seq(b, nc, variant, src, dst, s,
                               qT, kT, vsb, relk_ready)
                if self._tap_hit:
                    return

    def _attn_seq(self, b, nc, variant, src, dst, s, qT, kT, vsb, relk_ready):
        identb = self.identb
        is_a = variant == "a"
        tg = ("a" if is_a else "t") + str(s)
        NH = NH_A if is_a else NH_B
        HD = HD_A if is_a else HD_B
        NCH = 8
        row0 = s * SEQ
        niter = NH * NQT
        wq_in = self.waq if is_a else self.wbq
        wk_in = self.wak if is_a else self.wbk
        wv_in = self.wav if is_a else self.wbv
        wo_in = self.wao if is_a else self.wbo

        pe = "ape"
        f_q = "afq"
        f_o = "afo"

        with ExitStack() as p1:
            def sbuf1(name, shape, dtype):
                return p1.enter_context(nc.sbuf_tensor(tg + name, shape, dtype))

            wqs = sbuf1("wq", [P, NCH * DM], BF16)
            wks = sbuf1("wk", [P, NCH * DM], BF16)
            wvs = sbuf1("wv", [P, NCH * DM], BF16)
            xb = [sbuf1("xb0", [P, DM], F32), sbuf1("xb1", [P, DM], F32)]
            usq = sbuf1("usq", [P, DM], BF16)
            ub = [sbuf1("ub0", [P, DM], BF16), sbuf1("ub1", [P, DM], BF16)]
            u_fm = sbuf1("ufm", [P, NCH * GT], BF16)
            stats = sbuf1("stats", [P, 8], F32)
            psQK = [p1.enter_context(nc.psum_tensor(tg + "psQ0", [P, DM], F32)),
                    p1.enter_context(nc.psum_tensor(tg + "psQ1", [P, DM], F32))]
            psV = [p1.enter_context(nc.psum_tensor(tg + "psV0", [P, DM], F32)),
                   p1.enter_context(nc.psum_tensor(tg + "psV1", [P, DM], F32))]

            wtoks = []
            for dsrc, dstb, semn in ((wq_in, wqs, "alwq"),
                                     (wk_in, wks, "alwk"),
                                     (wv_in, wvs, "alwv")):
                ap = bass.AP(dsrc, 0, [[DM, P], [P * DM, NCH], [1, DM]])
                wtoks.append(b.dma("gpsimd", dstb[:].rearrange(
                    "p (ic o) -> p ic o", ic=NCH), ap, semn))
            wq_tok, wk_tok, wv_tok = wtoks

            x_sem = ["axl0", "axl1"]
            ln_last = [None, None]
            tp_done = {}
            ufm_evac = {}
            qk_evac = {}
            v_evac = {}
            last_qkv_read_ufm = None
            for g in range(4):
                for t in range(4):
                    ti = g * 4 + t
                    slot = ti % 2
                    xt = xb[slot][:]
                    w = [ln_last[slot]] if ln_last[slot] else []
                    xld = b.dma("gpsimd", xt,
                                src.ap()[row0 + ti * P:row0 + (ti + 1) * P, :],
                                x_sem[slot], wait=w)
                    s_sum = stats[:, 0:1]
                    s_sq = stats[:, 1:2]
                    s_m = stats[:, 2:3]
                    s_v = stats[:, 3:4]
                    s_r0 = stats[:, 4:5]
                    s_r = stats[:, 5:6]
                    s_m2 = stats[:, 6:7]
                    s_nb = stats[:, 7:8]
                    aw = [xld]
                    a1 = b.op("scalar",
                              lambda e, xt=xt, s_sq=s_sq: e.activation(
                                  usq[:], xt, AF.Square, accum_out=s_sq),
                              wait=aw)
                    b.op("vector",
                         lambda e, xt=xt, s_sum=s_sum: e.reduce_sum(
                             s_sum, xt, AX.X), wait=aw)
                    b.op("vector", lambda e, s_m=s_m, s_sum=s_sum:
                         e.tensor_scalar_mul(s_m, s_sum, 1.0 / DM))
                    b.op("vector", lambda e, s_v=s_v, s_sq=s_sq:
                         e.tensor_scalar_mul(s_v, s_sq, 1.0 / DM),
                         wait=[a1])
                    b.op("vector", lambda e, s_m2=s_m2, s_m=s_m:
                         e.tensor_mul(s_m2, s_m, s_m))
                    d2 = b.op("vector", lambda e, s_v=s_v, s_m2=s_m2:
                              e.tensor_sub(s_v, s_v, s_m2))
                    a2 = b.op("scalar", lambda e, s_v=s_v, s_r0=s_r0:
                              e.activation(s_r0, s_v, AF.Sqrt,
                                           bias=self.constv[:, 0:1]),
                              wait=[d2])
                    b.op("vector", lambda e, s_r=s_r, s_r0=s_r0:
                         e.reciprocal(s_r, s_r0), wait=[a2])
                    d4 = b.op("vector", lambda e, s_nb=s_nb, s_m=s_m, s_r=s_r:
                              e.tensor_scalar(s_nb, s_m, s_r, -1.0, ALU.mult,
                                              ALU.mult))
                    uw = [d4]
                    if (ti - 2) >= 0 and (ti - 2) in tp_done:
                        uw.append(tp_done[ti - 2])
                    du = b.op("vector",
                              lambda e, xt=xt, slot=slot, s_r=s_r, s_nb=s_nb:
                              e.tensor_scalar(ub[slot][:], xt, s_r, s_nb,
                                              ALU.mult, ALU.add),
                              wait=uw)
                    ln_last[slot] = du
                    tw = [du]
                    if (ti - 2) in ufm_evac:
                        tw.append(ufm_evac[ti - 2])
                    if t == 0 and g > 0:
                        tw.append(qk_evac[(1, NCH - 2, g - 1)])
                        tw.append(qk_evac[(1, NCH - 1, g - 1)])
                    tp = None
                    for ic in range(NCH):
                        tp = b.op("tensor",
                                  lambda e, t=t, ic=ic, slot=slot: e.matmul(
                                      psQK[t % 2][:, ic * P:(ic + 1) * P],
                                      ub[slot][:, ic * P:(ic + 1) * P],
                                      identb[:], start=True, stop=True),
                                  wait=tw if ic == 0 else [],
                                  inc=pe if ic == NCH - 1 else None)
                    tp_done[ti] = tp
                    ufm_out = u_fm[:].rearrange(
                        "p (ic gg) -> p ic gg", ic=NCH)[:, :, t * P:(t + 1) * P]
                    pt_src = psQK[t % 2][:].rearrange(
                        "p (a q) -> p a q", a=NCH)
                    ufm_evac[ti] = b.op(
                        "vector",
                        lambda e, ufm_out=ufm_out, pt_src=pt_src:
                        e.tensor_copy(ufm_out, pt_src),
                        wait=[tp])
                ufm_ready = ufm_evac[g * 4 + 3]

                for which, wbuf, wtok, dstT, scale_q in (
                        (0, wqs, wq_tok, qT, True),
                        (1, wks, wk_tok, kT, False)):
                    for oc in range(NCH):
                        psp = psQK[oc % 2][:, which * GT:(which + 1) * GT]
                        mw = [ufm_ready, wtok]
                        pk = ((which, oc - 2, g) if oc >= 2
                              else (1, NCH - 2 + oc, g - 1))
                        if pk in qk_evac:
                            mw.append(qk_evac[pk])
                        mtok = None
                        for ic in range(NCH):
                            mtok = b.op(
                                "tensor",
                                lambda e, oc=oc, ic=ic, wbuf=wbuf, psp=psp:
                                e.matmul(
                                    psp,
                                    wbuf[:, ic * DM + oc * P:
                                         ic * DM + (oc + 1) * P],
                                    u_fm[:, ic * GT:(ic + 1) * GT],
                                    start=(ic == 0), stop=(ic == NCH - 1)),
                                wait=mw if ic == 0 else [],
                                inc=f_q if ic == NCH - 1 else None)
                        qcol = 1 if is_a else 2
                        sc = self.constv[:, qcol:qcol + 1] if scale_q else 1.0
                        qk_evac[(which, oc, g)] = b.op(
                            "scalar",
                            lambda e, oc=oc, g=g, psp=psp, sc=sc, dstT=dstT:
                            e.activation(
                                dstT[:, oc * SEQ + g * GT:
                                     oc * SEQ + (g + 1) * GT],
                                psp, AF.Copy, scale=sc),
                            wait=[mtok])
                for t in range(4):
                    ti = g * 4 + t
                    psp = psV[t % 2]
                    mw = [ufm_ready, wv_tok]
                    pk = ti - 2
                    if pk in v_evac:
                        mw.append(v_evac[pk])
                    mtok = None
                    for ic in range(NCH):
                        for half in range(2):
                            mtok = b.op(
                                "tensor",
                                lambda e, t=t, ic=ic, half=half, psp=psp:
                                e.matmul(
                                    psp[:, half * GT:(half + 1) * GT],
                                    u_fm[:, ic * GT + t * P:
                                         ic * GT + (t + 1) * P],
                                    wvs[:, ic * DM + half * GT:
                                        ic * DM + (half + 1) * GT],
                                    start=(ic == 0), stop=(ic == NCH - 1)),
                                wait=mw if (ic == 0 and half == 0) else [],
                                inc=f_q if (ic == NCH - 1 and half == 1)
                                else None)
                    last_qkv_read_ufm = mtok
                    v_evac[ti] = b.op(
                        "vector",
                        lambda e, ti=ti, psp=psp: e.tensor_copy(
                            vsb[:, ti * DM:(ti + 1) * DM], psp[:]),
                        wait=[mtok])
            qkv_ready = [qk_evac[(0, NCH - 1, 3)], qk_evac[(1, NCH - 1, 3)],
                         v_evac[15]]
        if self.tap == ("qkv" + variant, s):
            self._tapd(qT, NCH * SEQ, qkv_ready)
            self._tapd(kT, NCH * SEQ, qkv_ready, row0=16)
            return

        with ExitStack() as p23:
            afm = p23.enter_context(
                nc.sbuf_tensor(tg + "afm", [P, NCH * SEQ], BF16))
            attn_done = self._attn_scores(
                b, nc, variant, s, qT, kT, vsb, afm, qkv_ready, relk_ready)
            if self._tap_hit:
                return

            with ExitStack() as p3:
                def sbuf3(name, shape, dtype):
                    return p3.enter_context(
                        nc.sbuf_tensor(tg + name, shape, dtype))

                wos = sbuf3("wo", [P, NCH * DM], BF16)
                xres = [sbuf3("xr0", [P, DM], F32), sbuf3("xr1", [P, DM], F32)]
                outr = [sbuf3("or0", [P, DM], F32), sbuf3("or1", [P, DM], F32)]
                psW = [p3.enter_context(
                           nc.psum_tensor(tg + "psW0", [P, DM], F32)),
                       p3.enter_context(
                           nc.psum_tensor(tg + "psW1", [P, DM], F32))]
                wo_tok = b.dma("gpsimd", wos[:].rearrange(
                    "p (ic o) -> p ic o", ic=NCH),
                    bass.AP(wo_in, 0, [[DM, P], [P * DM, NCH], [1, DM]]),
                    "alwo", wait=[attn_done])
                xr_sem = ["axr0", "axr1"]
                add_tok = {}
                st_tok = {}
                for qt in range(NQT):
                    slot = qt % 2
                    w = [add_tok[qt - 2]] if qt >= 2 else [attn_done]
                    xld = b.dma("gpsimd", xres[slot][:],
                                src.ap()[row0 + qt * P:row0 + (qt + 1) * P, :],
                                xr_sem[slot], wait=w)
                    mw = [attn_done, wo_tok]
                    if qt >= 2:
                        mw.append(add_tok[qt - 2])
                    mtok = None
                    for ic in range(NCH):
                        for half in range(2):
                            mtok = b.op(
                                "tensor",
                                lambda e, qt=qt, ic=ic, half=half, slot=slot:
                                e.matmul(
                                    psW[slot][:, half * GT:(half + 1) * GT],
                                    afm[:, ic * SEQ + qt * P:
                                        ic * SEQ + (qt + 1) * P],
                                    wos[:, ic * DM + half * GT:
                                        ic * DM + (half + 1) * GT],
                                    start=(ic == 0), stop=(ic == NCH - 1)),
                                wait=mw if (ic == 0 and half == 0) else [],
                                inc=f_o if (ic == NCH - 1 and half == 1)
                                else None)
                    aw = [mtok, xld]
                    if qt >= 2 and (qt - 2) in st_tok:
                        aw.append(st_tok[qt - 2])
                    if is_a:
                        add_tok[qt] = b.op(
                            "vector",
                            lambda e, slot=slot: e.tensor_add(
                                outr[slot][:], psW[slot][:], xres[slot][:]),
                            wait=aw)
                    else:
                        t0 = b.op("vector",
                                  lambda e, slot=slot: e.tensor_add(
                                      outr[slot][:], psW[slot][:],
                                      self.boutbc[:]),
                                  wait=aw)
                        add_tok[qt] = b.op(
                            "vector",
                            lambda e, slot=slot: e.tensor_add(
                                outr[slot][:], outr[slot][:], xres[slot][:]),
                            wait=[t0])
                    st_tok[qt] = b.dma(
                        "gpsimd",
                        dst.ap()[row0 + qt * P:row0 + (qt + 1) * P, :],
                        outr[slot][:], "ast" + str(slot),
                        wait=[add_tok[qt]])
                b.wait("gpsimd", [st_tok[NQT - 1], st_tok[NQT - 2]])

    def _attn_scores(self, b, nc, variant, s, qT, kT, vsb, afm, qkv_ready,
                     relk_ready):
        identb = self.identb
        is_a = variant == "a"
        tg = ("a" if is_a else "t") + str(s) + "s"
        NH = NH_A if is_a else NH_B
        niter = NH * NQT
        f_s = "afs"
        f_t = "aft"
        f_v = "afv"
        f_m = "afm_"

        NPB = 3 if is_a else 2
        with ExitStack() as p2:
            def sbuf2(name, shape, dtype):
                return p2.enter_context(nc.sbuf_tensor(tg + name, shape, dtype))

            St_sb = [sbuf2("St0", [P, SEQ], BF16), sbuf2("St1", [P, SEQ], BF16)]
            P_sb = [sbuf2("P%d" % k, [P, SEQ], BF16) for k in range(NPB)]
            PT_sb = [sbuf2("PT%d" % k, [P, SEQ], BF16) for k in range(NPB)]
            sst = [sbuf2("ss%d" % k, [P, 4], F32) for k in range(NPB)]
            ps_S = p2.enter_context(nc.psum_tensor(tg + "psS", [P, SEQ], F32))
            ps_Tb = p2.enter_context(nc.psum_tensor(tg + "psT", [P, SEQ], BF16))
            ps_PV = p2.enter_context(nc.psum_tensor(tg + "psPV", [P, P], F32))
            if not is_a:
                ps_Mt = p2.enter_context(
                    nc.psum_tensor(tg + "psMt", [P, P], F32))
                M_sb = [sbuf2("M0", [P, RWIN], BF16),
                        sbuf2("M1", [P, RWIN], BF16)]
                qa_sb = [sbuf2("qa0", [P, P], BF16), sbuf2("qa1", [P, P], BF16)]
                qb_sb = [sbuf2("qb0", [P, P], BF16), sbuf2("qb1", [P, P], BF16)]
                relk = [sbuf2("rk0", [P, NPOS], BF16),
                        sbuf2("rk1", [P, NPOS], BF16)]
                relk_sem = ["ark0", "ark1"]
                relk_tok = [None, None]
                mw_sem = ["amw0", "amw1"]
                ac_sem = ["aac0", "aac1"]
                mring_free = {}
                msb_free = [None, None]
                rs_tok = {}
                mev_tok = {}
                qbias_free = {0: [None, None], 1: [None, None]}

            s_tok = {}
            ev_tok = {}
            exp_tok = {}
            norm_tok = {}
            tp_tok = {}
            ptev_tok = {}
            pv_tok = {}
            pvev_tok = {}

            def idx_hq(i):
                return i // NQT, i % NQT

            def emit_qbias(i, which):
                h, qt = idx_hq(i)
                buf = (qa_sb if which == 0 else qb_sb)[i % 2]
                bias = (self.rebs if which == 0 else self.rpbs)[:, h:h + 1]
                gate = list(qkv_ready)
                if qbias_free[which][i % 2]:
                    gate.append(qbias_free[which][i % 2])
                return b.op("vector",
                            lambda e, h=h, qt=qt, buf=buf, bias=bias:
                            e.tensor_scalar_add(
                                buf[:], qT[:, h * SEQ + qt * P:
                                           h * SEQ + (qt + 1) * P], bias),
                            wait=gate)

            def emit_S(i):
                h, qt = idx_hq(i)
                is_a_l = is_a
                if is_a:
                    ch, rb = h // 2, 64 * (h % 2)
                    lhs_fn = (lambda ch=ch, rb=rb, qt=qt:
                              qT[rb:rb + 64, ch * SEQ + qt * P:
                                 ch * SEQ + (qt + 1) * P])
                    rhs_fn = (lambda c, ch=ch, rb=rb: kT[
                        rb:rb + 64,
                        ch * SEQ + c * 512:ch * SEQ + (c + 1) * 512])
                    qa_tok = None
                else:
                    qa_tok = emit_qbias(i, 0)
                    lhs_fn = lambda i=i: qa_sb[i % 2][:]
                    rhs_fn = (lambda c, h=h: kT[
                        :, h * SEQ + c * 512:h * SEQ + (c + 1) * 512])
                toks = []
                for c in range(4):
                    w = []
                    if c == 0:
                        w = list(qkv_ready)
                        if qa_tok:
                            w.append(qa_tok)
                        if not is_a and i + 1 in mev_tok:
                            w += list(mev_tok[i + 1][:2])
                        elif i >= 1:
                            w += list(ev_tok[i - 1])
                    toks.append(b.op(
                        "tensor",
                        lambda e, c=c, lhs_fn=lhs_fn, rhs_fn=rhs_fn:
                        e.matmul(ps_S[:, c * 512:(c + 1) * 512],
                                 lhs_fn(), rhs_fn(c),
                                 start=True, stop=True),
                        wait=w, inc=(f_s if c == 3 else None)))
                if not is_a:
                    qbias_free[0][i % 2] = toks[-1]
                s_tok[i] = toks[-1]

            def emit_M(j):
                h, qt = idx_hq(j)
                r_lo = (SEQ - P) - qt * P
                qb_tok = emit_qbias(j, 1)
                slot_rk = h % 2
                mslot = j % 2
                dslot = j % self.NMRING
                mtok = None
                for c in range(4):
                    w = []
                    if c == 0:
                        w = [qb_tok, relk_tok[slot_rk]]
                        if (j - 2) >= 0 and (j - 2) in ev_tok:
                            w += list(ev_tok[j - 2])
                        elif j == 1:
                            w += [mev_tok[0][0], mev_tok[0][1]]
                    mtok = b.op(
                        "tensor",
                        lambda e, c=c, mslot=mslot, r_lo=r_lo,
                        slot_rk=slot_rk: e.matmul(
                            ps_S[:, c * 512:(c + 1) * 512],
                            qb_sb[mslot][:],
                            relk[slot_rk][:, r_lo + c * 512:
                                          r_lo + (c + 1) * 512],
                            start=True, stop=True),
                        wait=w, inc=f_m if c == 3 else None)
                mt_tok = b.op(
                    "tensor",
                    lambda e, mslot=mslot, r_lo=r_lo, slot_rk=slot_rk:
                    e.matmul(
                        ps_Mt[:, 0:127],
                        qb_sb[mslot][:],
                        relk[slot_rk][:, r_lo + 2048:r_lo + 2175],
                        start=True, stop=True),
                    wait=[mev_tok[j - 1][2]] if j >= 1 else [],
                    inc=f_m)
                qbias_free[1][j % 2] = mt_tok
                w1_ = [mtok]
                if msb_free[mslot]:
                    w1_.append(msb_free[mslot])
                me_d = b.op("vector",
                            lambda e, mslot=mslot: e.tensor_copy(
                                M_sb[mslot][:, 0:1024], ps_S[:, 0:1024]),
                            wait=w1_)
                me_a = b.op("scalar",
                            lambda e, mslot=mslot: e.activation(
                                M_sb[mslot][:, 1024:2048], ps_S[:, 1024:2048],
                                AF.Copy),
                            wait=[mtok] + ([msb_free[mslot]]
                                           if msb_free[mslot] else []))
                me_t = b.op("scalar",
                            lambda e, mslot=mslot: e.activation(
                                M_sb[mslot][:, 2048:2175], ps_Mt[:, 0:127],
                                AF.Copy),
                            wait=[mt_tok])
                mev_tok[j] = (me_d, me_a, me_t)
                wgate = [me_d, me_a, me_t]
                if dslot in mring_free:
                    wgate.append(mring_free[dslot])
                wr = b.dma("gpsimd",
                           bass.AP(self.m_ring, dslot * P * RWIN,
                                   [[RWIN, P], [1, RWIN]]),
                           M_sb[mslot][:], mw_sem[mslot], wait=wgate)
                msb_free[mslot] = wr
                return wr

            wr_tok = {}
            mx_tok = {}

            def emit_ev(j):
                par = j % 2
                w = [s_tok[j]]
                if j >= 2:
                    w.append(exp_tok[j - 2])
                ev = b.op("scalar",
                          lambda e, par=par: e.activation(
                              St_sb[par][:], ps_S[:], AF.Copy),
                          wait=w)
                ev_tok[j] = (ev,)

            def emit_mx(j):
                spar = j % 2
                mneg = sst[j % NPB][:, 0:1]
                w = list(ev_tok[j]) if is_a else [rs_tok[j]]
                mx_tok[j] = b.op(
                    "vector",
                    lambda e, spar=spar, mneg=mneg: e.reduce_max(
                        mneg, St_sb[spar][:], AX.X, negate=True),
                    wait=w)

            def emit_rd(j):
                dslot = j % self.NMRING
                rw = [wr_tok[j]] + list(ev_tok[j])
                rd = b.dma("gpsimd", St_sb[j % 2][:],
                           bass.AP(self.m_ring, dslot * P * RWIN + 127,
                                   [[RWIN - 1, P], [1, SEQ]]),
                           ac_sem[j % 2], wait=rw, accum_op=ALU.add)
                rs_tok[j] = rd
                mring_free[dslot] = rd

            if not is_a:
                relk_tok[0] = b.dma(
                    "gpsimd", relk[0][:],
                    bass.AP(self.c_relkT, 0, [[NPOS, P], [1, NPOS]]),
                    relk_sem[0], wait=[relk_ready] + list(qkv_ready))
                wr_tok[0] = emit_M(0)
                if niter > 1:
                    wr_tok[1] = emit_M(1)
            emit_S(0)
            emit_ev(0)
            if not is_a:
                emit_rd(0)
            emit_mx(0)
            if not is_a and niter > 2:
                wr_tok[2] = emit_M(2)
            if niter > 1:
                emit_S(1)

            for i in range(niter + 2):
                par = i % NPB
                spar = i % 2
                if not is_a:
                    j4 = i + 4
                    if j4 < niter and j4 % NQT == 0:
                        hh = j4 // NQT
                        gate = []
                        j_last = (hh - 2) * NQT + NQT - 1
                        if j_last in mev_tok:
                            gate = [mev_tok[j_last][2]]
                        elif hh == 1:
                            gate = list(qkv_ready)
                        relk_tok[hh % 2] = b.dma(
                            "gpsimd", relk[hh % 2][:],
                            bass.AP(self.c_relkT, hh * P * NPOS,
                                    [[NPOS, P], [1, NPOS]]),
                            relk_sem[hh % 2], wait=gate)

                if i + 1 < niter:
                    emit_ev(i + 1)
                    if not is_a:
                        emit_rd(i + 1)
                    emit_mx(i + 1)
                if not is_a and i + 3 < niter:
                    wr_tok[i + 3] = emit_M(i + 3)
                if i + 2 < niter:
                    emit_S(i + 2)

                if i < niter:
                    h, qt = idx_hq(i)
                    ch = h // 2 if is_a else h
                    lsum = sst[par][:, 1:2]
                    rl = sst[par][:, 2:3]
                    mneg = sst[i % NPB][:, 0:1]
                    ew = [mx_tok[i]]
                    if i >= NPB:
                        ew.append(tp_tok[i - NPB])
                    ex = b.op("scalar",
                              lambda e, par=par, spar=spar, mneg=mneg,
                              lsum=lsum:
                              e.activation(P_sb[par][:], St_sb[spar][:],
                                           AF.Exp, bias=mneg,
                                           accum_out=lsum),
                              wait=ew)
                    exp_tok[i] = ex
                    rc = b.op("vector",
                              lambda e, rl=rl, lsum=lsum: e.reciprocal(
                                  rl, lsum),
                              wait=[ex])
                    nt = b.op("vector",
                              lambda e, par=par, rl=rl: e.tensor_scalar_mul(
                                  P_sb[par][:], P_sb[par][:], rl),
                              wait=[rc])
                    norm_tok[i] = nt

                if 2 <= i <= niter + 1:
                    hpp, qtpp = idx_hq(i - 2)
                    chpp = hpp // 2 if is_a else hpp
                    if is_a:
                        dest = afm[64 * (hpp % 2):64 * (hpp % 2) + 64,
                                   chpp * SEQ + qtpp * P:
                                   chpp * SEQ + (qtpp + 1) * P]
                        srcp = ps_PV[0:64, :]
                    else:
                        dest = afm[:, chpp * SEQ + qtpp * P:
                                   chpp * SEQ + (qtpp + 1) * P]
                        srcp = ps_PV[:, :]
                    pvev_tok[i - 2] = b.op(
                        "scalar",
                        lambda e, dest=dest, srcp=srcp: e.activation(
                            dest, srcp, AF.Copy),
                        wait=[pv_tok[i - 2]])

                if 1 <= i <= niter:
                    hp, qtp = idx_hq(i - 1)
                    chp = hp // 2 if is_a else hp
                    parp = (i - 1) % NPB
                    w = [tp_tok[i - 1]]
                    if (i - 1 - NPB) >= 0:
                        w.append(pv_tok[i - 1 - NPB])
                    ptev_tok[i - 1] = b.op(
                        "vector",
                        lambda e, parp=parp: e.tensor_copy(
                            PT_sb[parp][:], ps_Tb[:]),
                        wait=w)

                if i < niter:
                    ttok = None
                    for kb in range(NKB):
                        w = []
                        if kb == 0:
                            w = [norm_tok[i]]
                            if i >= 1:
                                w.append(ptev_tok[i - 1])
                        ttok = b.op(
                            "tensor",
                            lambda e, kb=kb, par=par: e.transpose(
                                ps_Tb[:, kb * P:(kb + 1) * P],
                                P_sb[par][:, kb * P:(kb + 1) * P],
                                identb[:]),
                            wait=w, inc=f_t if kb == NKB - 1 else None)
                    tp_tok[i] = ttok

                if 1 <= i <= niter:
                    ptok = None
                    for kb in range(NKB):
                        w = []
                        if kb == 0:
                            w = [ptev_tok[i - 1]]
                            if i >= 2:
                                w.append(pvev_tok[i - 2])
                        if is_a:
                            lhs = (lambda kb=kb, hp=hp:
                                   vsb[:, kb * DM + hp * HD_A:
                                       kb * DM + (hp + 1) * HD_A])
                            out = ps_PV[0:64, :]
                        else:
                            lhs = (lambda kb=kb, hp=hp:
                                   vsb[:, kb * DM + hp * HD_B:
                                       kb * DM + (hp + 1) * HD_B])
                            out = ps_PV[:, :]
                        ptok = b.op(
                            "tensor",
                            lambda e, kb=kb, lhs=lhs, out=out, parp=parp:
                            e.matmul(
                                out, lhs(),
                                PT_sb[parp][:, kb * P:(kb + 1) * P],
                                start=(kb == 0), stop=(kb == NKB - 1)),
                            wait=w, inc=f_v if kb == NKB - 1 else None)
                    pv_tok[i - 1] = ptok

            if self.tap == ("afm" + variant, s):
                self._tapd(afm, 8 * SEQ, [pvev_tok[niter - 1]])
                return None
            return pvev_tok[niter - 1]

def kernel(**inputs):
    ins = _prepare_inputs(inputs)
    prog = Prog()
    nc = prog.build()
    in_maps = []
    for c in range(8):
        m = dict(ins["common"])
        m["x"] = ins["x_shards"][c]
        in_maps.append(m)
    res = run_bass_kernel_spmd(nc, in_maps, core_ids=list(range(8)))
    out = np.concatenate(
        [np.asarray(r["y"]).reshape(NSEQ, SEQ, DM) for r in res.results], axis=0)
    return out.astype(np.float32)


def _prepare_inputs(d):
    x = np.asarray(d["L0_attn_a_hidden_states"], np.float32)
    x_shards = [np.ascontiguousarray(x[2 * c:2 * c + 2].reshape(TOK, DM))
                for c in range(8)]

    def T(a):
        return np.ascontiguousarray(np.asarray(a, np.float32).T)

    def lay_s(s):
        return np.ascontiguousarray(
            np.asarray(s, np.float32).reshape(NOC_DH, P).T)

    common = {
        "waqT": T(d["L0_attn_a_Wq"]),
        "wakT": T(d["L0_attn_a_Wk"]),
        "wavT": T(d["L0_attn_a_Wv"]),
        "waoT": T(d["L0_attn_a_Wo"]),
        "wbq": np.ascontiguousarray(np.asarray(d["L0_attn_b_w_qs"], np.float32)),
        "wbk": np.ascontiguousarray(np.asarray(d["L0_attn_b_w_ks"], np.float32)),
        "wbv": np.ascontiguousarray(np.asarray(d["L0_attn_b_w_vs"], np.float32)),
        "wbo": np.ascontiguousarray(np.asarray(d["L0_attn_b_w_out"], np.float32)),
        "wbrel": np.ascontiguousarray(
            np.asarray(d["L0_attn_b_w_rel_pos_k"], np.float32)),
        "wbreb": np.ascontiguousarray(
            np.asarray(d["L0_attn_b_rel_embed_bias"], np.float32).reshape(
                NH_B, HD_B)),
        "wbrpb": np.ascontiguousarray(
            np.asarray(d["L0_attn_b_rel_pos_bias"], np.float32).reshape(
                NH_B, HD_B)),
        "boutrow": np.ascontiguousarray(
            np.asarray(d["L0_attn_b_b_out"], np.float32).reshape(1, DM)),
        "posT": _positions_T(),
        "w1T": T(d["L0_ffn_w1_w"]),
        "w2T": T(d["L0_ffn_w2_w"]),
        "h1T": T(d["L0_ffn_h1_w"]),
        "h2T": T(d["L0_ffn_h2_w"]),
        "w3T": T(d["L0_ffn_w3_w"]),
        "s1l": lay_s(d["L0_ffn_w1_s"]),
        "s2l": lay_s(d["L0_ffn_w2_s"]),
        "hs1l": lay_s(d["L0_ffn_h1_s"]),
        "hs2l": lay_s(d["L0_ffn_h2_s"]),
        "s3row": np.ascontiguousarray(
            np.asarray(d["L0_ffn_w3_s"], np.float32).reshape(1, DM)),
    }
    return {"x_shards": x_shards, "common": common}


# version 11
# speedup vs baseline: 1.0408x; 1.0408x over previous
import math
from contextlib import ExitStack

import numpy as np
import concourse.bass as bass
import concourse.mybir as mybir
from concourse.bass_utils import run_bass_kernel_spmd

F32 = mybir.dt.float32
BF16 = mybir.dt.bfloat16
AF = mybir.ActivationFunctionType
ALU = mybir.AluOpType
AX = mybir.AxisListType

P = 128
DM = 1024
DH = 4096
SEQ = 2048
NSEQ = 2
TOK = SEQ * NSEQ
NT = TOK // P
GT = 512
NGRP = TOK // GT
NIC_DM = DM // P
NOC_DH = DH // P
NH_A = 16
HD_A = 64
NH_B = 8
HD_B = 128
RWIN = SEQ + P - 1
NPOS = 2 * SEQ - 1
RFEAT = 192
EPS = 1e-5
NQT = SEQ // P
NKB = SEQ // P

ALL_ENGINES = ("tensor", "vector", "scalar", "gpsimd", "sync")
AUTO_ENGINES = ("vector", "scalar", "gpsimd")


class Builder:
    def __init__(self, nc):
        self.nc = nc
        self.streams = {e: [] for e in ALL_ENGINES}
        self.sem_order = []
        self.counts = {}
        self.sems = {}
        self.watermark = {e: {} for e in ALL_ENGINES}

    def sem(self, name):
        if name not in self.counts:
            self.sem_order.append(name)
            self.counts[name] = 0
        return name

    def _filter(self, engine, wait):
        wl = []
        wm = self.watermark[engine]
        best = {}
        for s, v in wait:
            if v is None or v <= 0:
                continue
            best[s] = max(best.get(s, 0), v)
        for s, v in best.items():
            if wm.get(s, 0) >= v:
                continue
            wl.append((s, v))
            wm[s] = v
        return wl

    def op(self, engine, fn, wait=(), inc=None, auto=True):
        wl = self._filter(engine, wait)
        tok = None
        incs = []
        if engine in AUTO_ENGINES and auto:
            assert inc is None
            a = self.sem("au_" + engine)
            prev = self.counts[a]
            wm = self.watermark[engine]
            if prev > 0 and wm.get(a, 0) < prev:
                wl.append((a, prev))
                wm[a] = prev
            self.counts[a] += 1
            tok = (a, self.counts[a])
            incs = [(a, 1)]
        elif inc is not None:
            s = self.sem(inc)
            self.counts[s] += 1
            tok = (s, self.counts[s])
            incs = [(s, 1)]
        self.streams[engine].append((fn, tuple(wl), tuple(incs)))
        return tok

    def dma(self, queue, out_ap, in_ap, sem, wait=(), **kw):
        s = self.sem(sem)
        prev = self.counts[s]
        wl = self._filter(queue, list(wait) + ([(s, prev)] if prev else []))
        self.counts[s] += 16
        tok = (s, self.counts[s])
        self.streams[queue].append((
            lambda e, o=out_ap, i=in_ap, kw=kw: e.dma_start(o, i, **kw),
            tuple(wl), ((s, 16),)))
        return tok

    def barrier(self):
        s = self.sem("bar")
        base = self.counts[s]
        for e in ALL_ENGINES:
            w = []
            if e in AUTO_ENGINES:
                a = self.sem("au_" + e)
                if self.counts[a] > 0:
                    w.append((a, self.counts[a]))
            self.counts[s] += 1
            self.streams[e].append((lambda eng: eng.nop(), tuple(w),
                                    ((s, 1),)))
        target = base + len(ALL_ENGINES)
        for e in ALL_ENGINES:
            self.streams[e].append((None, ((s, target),), ()))
            self.watermark[e] = {s: target}
        return target

    def wait(self, engine, tokens):
        wl = self._filter(engine, tokens)
        if wl:
            self.streams[engine].append((None, tuple(wl), ()))

    def emit(self, stack):
        nc = self.nc
        for name in self.sem_order:
            self.sems[name] = stack.enter_context(nc.semaphore(name))
        block = stack.enter_context(nc.Block())

        def make(engine_name):
            def _(eng):
                for fn, wait, inc in self.streams[engine_name]:
                    for s, v in wait:
                        eng.wait_ge(self.sems[s], v)
                    if fn is None:
                        continue
                    inst = fn(eng)
                    for s, a in inc:
                        inst.then_inc(self.sems[s], a)
            return _

        block.tensor(make("tensor"))
        block.vector(make("vector"))
        block.scalar(make("scalar"))
        block.gpsimd(make("gpsimd"))
        block.sync(make("sync"))


class Ring:

    def __init__(self, b, name, bufs, queue="sync"):
        self.b = b
        self.bufs = bufs
        self.n = len(bufs)
        self.queue = queue
        self.semn = [b.sem(f"{name}{i}") for i in range(self.n)]
        self.free_gate = [None] * self.n
        self.u = 0

    def load(self, in_ap, extra_wait=()):
        slot = self.u % self.n
        w = list(extra_wait)
        if self.free_gate[slot] is not None:
            w.append(self.free_gate[slot])
        tok = self.b.dma(self.queue, self.bufs[slot][:], in_ap,
                         self.semn[slot], wait=w)
        self.u += 1
        return slot, tok

    def set_free(self, slot, gate):
        self.free_gate[slot] = gate


def _positions_T():
    from scipy.special import gammaln

    seq_len, feature_size = SEQ, RFEAT
    distances = np.arange(-seq_len + 1, seq_len, dtype=np.float64)
    n = feature_size // 6
    absd = np.abs(distances)[:, None]
    max_range = math.log(seq_len) / math.log(2.0)
    half_life = 2.0 ** np.linspace(3.0, max_range, n)
    feat_exp = np.exp(-math.log(2.0) / half_life[None, :] * absd)
    center_widths = 2.0 ** np.arange(1, n + 1, dtype=np.float64) - 1.0
    feat_cm = (center_widths[None, :] > absd).astype(np.float64)
    stddev = seq_len / (2.0 * n)
    start_mean = seq_len / n
    mean = np.linspace(start_mean, float(seq_len), n)[None, :]
    conc = (mean / stddev) ** 2
    rate = mean / stddev**2
    with np.errstate(divide="ignore", invalid="ignore"):
        xlogy = np.where(conc - 1.0 == 0.0, 0.0, (conc - 1.0) * np.log(absd))
    log_unnorm = xlogy - rate * absd
    log_norm = gammaln(conc) - conc * np.log(rate)
    probs = np.exp(log_unnorm - log_norm) + 1e-08
    feat_gamma = probs / np.max(probs, axis=-1, keepdims=True)
    emb = np.concatenate([feat_exp, feat_cm, feat_gamma], axis=-1)
    emb = np.concatenate([emb, np.sign(distances)[:, None] * emb], axis=-1)
    return np.ascontiguousarray(emb.T.astype(np.float32))


class Prog:
    def __init__(self, phases=("attn_a", "ffn1", "attn_b", "ffn2")):
        self.phases = phases
        self.tap = None
        self.tap_i = 0
        self._tap_hit = False
        nc = self.nc = bass.Bass("TRN2", target_bir_lowering=False)

        def din(name, shape):
            return nc.dram_tensor(name, shape, F32, kind="ExternalInput")

        self.x_in = din("x", [TOK, DM])
        self.waq = din("waqT", [DM, DM])
        self.wak = din("wakT", [DM, DM])
        self.wav = din("wavT", [DM, DM])
        self.wao = din("waoT", [DM, DM])
        self.wbq = din("wbq", [DM, DM])
        self.wbk = din("wbk", [DM, DM])
        self.wbv = din("wbv", [DM, DM])
        self.wbo = din("wbo", [DM, DM])
        self.wbrel = din("wbrel", [RFEAT, DM])
        self.wbreb = din("wbreb", [NH_B, HD_B])
        self.wbrpb = din("wbrpb", [NH_B, HD_B])
        self.boutrow = din("boutrow", [1, DM])
        self.posT = din("posT", [RFEAT, NPOS])
        self.w1T = din("w1T", [DM, DH])
        self.w2T = din("w2T", [DM, DH])
        self.h1T = din("h1T", [DH, DH])
        self.h2T = din("h2T", [DH, DH])
        self.w3T = din("w3T", [DH, DM])
        self.s1l = din("s1l", [P, NOC_DH])
        self.s2l = din("s2l", [P, NOC_DH])
        self.hs1l = din("hs1l", [P, NOC_DH])
        self.hs2l = din("hs2l", [P, NOC_DH])
        self.s3row = din("s3row", [1, DM])

        self.y_out = nc.dram_tensor("y", [TOK, DM], F32, kind="ExternalOutput")
        self.xs = nc.dram_tensor("xs", [TOK, DM], F32, kind="Internal")
        self.c_w1 = nc.dram_tensor("c_w1", [NOC_DH * P * DM], BF16, kind="Internal")
        self.c_w2 = nc.dram_tensor("c_w2", [NOC_DH * P * DM], BF16, kind="Internal")
        self.c_h1 = nc.dram_tensor("c_h1", [NOC_DH * P * DH], BF16, kind="Internal")
        self.c_h2 = nc.dram_tensor("c_h2", [NOC_DH * P * DH], BF16, kind="Internal")
        self.c_w3 = nc.dram_tensor("c_w3", [DH * DM], BF16, kind="Internal")
        self.c_relkT = nc.dram_tensor("c_relkT", [NH_B * P * NPOS], BF16,
                                      kind="Internal")
        self.NMRING = 6
        self.m_ring = nc.dram_tensor("m_ring", [self.NMRING * P * RWIN], BF16,
                                     kind="Internal")
        self.b = Builder(nc)

    def build(self):
        b, nc = self.b, self.nc
        with ExitStack() as stack:
            def sbuf(name, shape, dtype):
                return stack.enter_context(nc.sbuf_tensor(name, shape, dtype))

            self.identb = sbuf("identb", [P, P], BF16)
            self.s1s = sbuf("s1s", [P, NOC_DH], F32)
            self.s2s = sbuf("s2s", [P, NOC_DH], F32)
            self.hs1s = sbuf("hs1s", [P, NOC_DH], F32)
            self.hs2s = sbuf("hs2s", [P, NOC_DH], F32)
            self.boutbc = sbuf("boutbc", [P, DM], F32)
            self.constv = sbuf("constv", [P, 4], F32)
            self.rebs = sbuf("rebs", [P, NH_B], F32)
            self.rpbs = sbuf("rpbs", [P, NH_B], F32)
            self.dbgbuf = sbuf("dbgbuf", [P, 40], F32)

            self.emit_init(stack)
            b.barrier()

            if not self._tap_hit:
                if "attn_a" in self.phases:
                    self.emit_attn(stack, variant="a", src=self.x_in,
                                   dst=self.xs)
                else:
                    self.emit_copy(self.x_in, self.xs)
                b.barrier()

            if not self._tap_hit and "ffn1" in self.phases:
                self.emit_ffn(stack, src=self.xs, dst=self.xs, tag="f1")
                b.barrier()

            if not self._tap_hit and "attn_b" in self.phases:
                self.emit_attn(stack, variant="b", src=self.xs, dst=self.xs)
                b.barrier()

            if not self._tap_hit:
                if "ffn2" in self.phases:
                    self.emit_ffn(stack, src=self.xs, dst=self.y_out, tag="f2")
                else:
                    self.emit_copy(self.xs, self.y_out)
            b.barrier()
            if getattr(self, "dbg_snap", None) is not None:
                b.dma("gpsimd", self.y_out.ap()[0:P, 0:40],
                      self.dbgbuf[:], "dbgd",
                      wait=[self._dbg1, self._dbg2])
                b.wait("gpsimd", [("dbgd", b.counts["dbgd"])])
            b.emit(stack)
        return nc

    def _tapd(self, buf, width, wait, row0=0):
        b = self.b
        self._tap_hit = True
        last = None
        nrt = max(1, width // DM)
        for t in range(nrt):
            w = min(DM, width)
            last = b.dma("gpsimd",
                         self.y_out.ap()[(row0 + t) * P:(row0 + t + 1) * P, 0:w],
                         buf[:, t * DM:t * DM + w], "tapd", wait=wait)
        b.wait("gpsimd", [last])

    def emit_copy(self, src, dst):
        b = self.b
        last = None
        for t in range(NT):
            last = b.dma("gpsimd",
                         dst.ap()[t * P:(t + 1) * P, :],
                         src.ap()[t * P:(t + 1) * P, :], "cp")
        b.wait("gpsimd", [last])

    def emit_init(self, stack):
        b, nc = self.b, self.nc

        b.op("vector", lambda e: e.memset(self.constv[:, 0:1], EPS))
        b.op("vector", lambda e: e.memset(self.constv[:, 1:2], 0.125))
        b.op("vector", lambda e: e.memset(self.constv[:, 2:3],
                                          float(HD_B) ** -0.5))
        b.op("gpsimd", lambda e: e.memset(self.identb[:], 1.0))
        b.op("gpsimd", lambda e: e.affine_select(
            self.identb[:], self.identb[:], [[1, P]], ALU.is_equal, 0.0,
            base=0, channel_multiplier=-1))

        for src, buf in ((self.s1l, self.s1s), (self.s2l, self.s2s),
                         (self.hs1l, self.hs1s), (self.hs2l, self.hs2s)):
            b.dma("gpsimd", buf[:], src.ap(), "ig")
        for h in range(NH_B):
            b.dma("gpsimd", self.rebs[:, h:h + 1],
                  self.wbreb.ap()[h:h + 1, :], "ig")
            b.dma("gpsimd", self.rpbs[:, h:h + 1],
                  self.wbrpb.ap()[h:h + 1, :], "ig")

        with nc.sbuf_tensor("i_ones", [1, P], BF16) as ones_t, \
             nc.sbuf_tensor("i_brow", [1, DM], BF16) as brow, \
             nc.sbuf_tensor("i_s3r", [1, DM], F32) as s3r, \
             nc.sbuf_tensor("i_s3rb", [1, DM], BF16) as s3rb, \
             nc.sbuf_tensor("i_s3b", [P, DM], F32) as s3bc, \
             nc.sbuf_tensor("i_stgf0", [P, DM], F32) as stgf0, \
             nc.sbuf_tensor("i_stgf1", [P, DM], F32) as stgf1, \
             nc.sbuf_tensor("i_stgb0", [P, DH], BF16) as stgb0, \
             nc.sbuf_tensor("i_stgb1", [P, DH], BF16) as stgb1, \
             nc.sbuf_tensor("i_w3b0", [P, DM], BF16) as w3b0, \
             nc.sbuf_tensor("i_w3b1", [P, DM], BF16) as w3b1, \
             nc.psum_tensor("i_ps", [P, DM], F32) as ips:
            b.op("vector", lambda e: e.memset(ones_t[:], 1.0))
            tb = b.dma("gpsimd", brow[:], self.boutrow.ap(), "ibrow")
            ts = b.dma("gpsimd", s3r[:], self.s3row.ap(), "is3r")
            d0 = b.op("vector", lambda e: e.tensor_copy(s3rb[:], s3r[:]),
                      wait=[ts])
            b.op("tensor", lambda e: e.matmul(ips[:, 0:512], ones_t[:],
                                              brow[:, 0:512], start=True,
                                              stop=True),
                 wait=[tb, d0])
            mk = b.op("tensor", lambda e: e.matmul(ips[:, 512:1024], ones_t[:],
                                                   brow[:, 512:1024],
                                                   start=True, stop=True),
                      inc="ipe")
            d1 = b.op("vector", lambda e: e.tensor_copy(self.boutbc[:], ips[:]),
                      wait=[mk])
            b.op("tensor", lambda e: e.matmul(ips[:, 0:512], ones_t[:],
                                              s3rb[:, 0:512], start=True,
                                              stop=True), wait=[d1])
            mk = b.op("tensor", lambda e: e.matmul(ips[:, 512:1024], ones_t[:],
                                                   s3rb[:, 512:1024],
                                                   start=True, stop=True),
                      inc="ipe")
            s3_ready = b.op("vector", lambda e: e.tensor_copy(s3bc[:], ips[:]),
                            wait=[mk])

            stg = [stgb0, stgb1]
            stg_sem = ["istg0", "istg1"]
            cw_sem = ["icw0", "icw1"]
            stg_free = [None, None]
            u = 0
            for src_t, dst_t, I, O in ((self.w1T, self.c_w1, DM, DH),
                                       (self.w2T, self.c_w2, DM, DH),
                                       (self.h1T, self.c_h1, DH, DH),
                                       (self.h2T, self.c_h2, DH, DH)):
                noc = O // P
                for ic in range(I // P):
                    sl = u % 2
                    w = [stg_free[sl]] if stg_free[sl] else []
                    t = b.dma("gpsimd", stg[sl][:, 0:O],
                              src_t.ap()[ic * P:(ic + 1) * P, :],
                              stg_sem[sl], wait=w)
                    cwt = b.dma("gpsimd",
                                bass.AP(dst_t, ic * P,
                                        [[I, P], [P * I, noc], [1, P]]),
                                stg[sl][:, 0:O].rearrange(
                                    "p (oc o) -> p oc o", oc=noc),
                                cw_sem[sl], wait=[t])
                    stg_free[sl] = cwt
                    u += 1

            stf = [stgf0, stgf1]
            stf_sem = ["istf0", "istf1"]
            cw3_sem = ["icw30", "icw31"]
            stf_free = [None, None]
            w3o = [w3b0, w3b1]
            w3o_free = [None, None]
            for ic in range(DH // P):
                sl = ic % 2
                w = [stf_free[sl]] if stf_free[sl] else []
                t = b.dma("gpsimd", stf[sl][:],
                          self.w3T.ap()[ic * P:(ic + 1) * P, :],
                          stf_sem[sl], wait=w)
                dw = [t, s3_ready]
                if w3o_free[sl]:
                    dw.append(w3o_free[sl])
                d = b.op("vector",
                         lambda e, sl=sl: e.tensor_mul(w3o[sl][:], stf[sl][:],
                                                       s3bc[:]),
                         wait=dw)
                stf_free[sl] = d
                cwt = b.dma("gpsimd",
                            bass.AP(self.c_w3, ic * P * DM, [[DM, P], [1, DM]]),
                            w3o[sl][:], cw3_sem[sl], wait=[d])
                w3o_free[sl] = cwt

            totals = [(s, b.counts[s]) for s in
                      ("ig", "ibrow", "is3r", "istg0", "istg1", "istf0",
                       "istf1", "icw0", "icw1", "icw30", "icw31")
                      if s in b.counts]
            b.wait("gpsimd", totals)

    def emit_ffn(self, stack, src, dst, tag):
        b, nc = self.b, self.nc
        identb = self.identb
        s1s, s2s, hs1s, hs2s = self.s1s, self.s2s, self.hs1s, self.hs2s

        pe = "fpe"
        f_w1 = "ffw1"
        f_w2 = "ffw2"
        f_h = "ffh"
        f_w3 = "ffw3"

        with ExitStack() as st:
            def sbuf(name, shape, dtype):
                return st.enter_context(nc.sbuf_tensor(tag + name, shape, dtype))

            xg = [sbuf("xg0", [P, 4 * DM], F32), sbuf("xg1", [P, 4 * DM], F32)]
            outg = sbuf("outg", [P, 4 * DM], F32)
            usq = sbuf("usq", [P, DM], BF16)
            ub = [sbuf("ub0", [P, DM], BF16), sbuf("ub1", [P, DM], BF16)]
            u_fm = sbuf("ufm", [P, NIC_DM * GT], BF16)
            y1 = sbuf("y1", [P, NOC_DH * GT], BF16)
            y2 = sbuf("y2", [P, NOC_DH * GT], BF16)
            t1b = [sbuf("t1b0", [P, GT], BF16), sbuf("t1b1", [P, GT], BF16)]
            t2b = [sbuf("t2b0", [P, GT], BF16), sbuf("t2b1", [P, GT], BF16)]
            stats = sbuf("stats", [P, 32], F32)

            x_sem = ["fxl0", "fxl1"]
            r_w1 = Ring(b, "frw1",
                        [sbuf("w1b0", [P, DM], BF16), sbuf("w1b1", [P, DM], BF16)])
            r_w2 = Ring(b, "frw2",
                        [sbuf("w2b0", [P, DM], BF16), sbuf("w2b1", [P, DM], BF16)])
            r_h = Ring(b, "frh",
                       [sbuf("hb0", [P, DH], BF16), sbuf("hb1", [P, DH], BF16),
                        sbuf("hb2", [P, DH], BF16)])
            r_w3 = Ring(b, "frw3",
                        [sbuf("w3b0", [P, DM], BF16), sbuf("w3b1", [P, DM], BF16)])

            pW = [st.enter_context(nc.psum_tensor(tag + "pW0", [P, DM], F32)),
                  st.enter_context(nc.psum_tensor(tag + "pW1", [P, DM], F32))]
            pT = [st.enter_context(nc.psum_tensor(tag + "pT0", [P, DM], F32)),
                  st.enter_context(nc.psum_tensor(tag + "pT1", [P, DM], F32))]

            res_by_grp = {}
            out_tok = {}
            last_w2_read_ufm = None
            last_w3_read_y3 = None
            t1b_free = [None, None]
            ufm_evac = {}
            tp_done = {}
            res_last = None

            for grp in range(NGRP):
                slot = grp % 2
                rows = src.ap()[grp * GT:(grp + 1) * GT, :]
                rows_dst = dst.ap()[grp * GT:(grp + 1) * GT, :]
                w = []
                if grp >= 2:
                    w.append(res_by_grp[grp - 2])
                xld = b.dma("gpsimd",
                            xg[slot][:].rearrange("p (t d) -> p t d", t=4),
                            rows.rearrange("(t p) d -> p t d", p=P),
                            x_sem[slot], wait=w)

                for t in range(4):
                    xt = xg[slot][:, t * DM:(t + 1) * DM]
                    c0 = 8 * t
                    s_sum = stats[:, c0:c0 + 1]
                    s_sq = stats[:, c0 + 1:c0 + 2]
                    s_m = stats[:, c0 + 2:c0 + 3]
                    s_v = stats[:, c0 + 3:c0 + 4]
                    s_r0 = stats[:, c0 + 4:c0 + 5]
                    s_r = stats[:, c0 + 5:c0 + 6]
                    s_m2 = stats[:, c0 + 6:c0 + 7]
                    s_nb = stats[:, c0 + 7:c0 + 8]
                    aw = [xld]
                    a1 = b.op("scalar",
                              lambda e, xt=xt, s_sq=s_sq: e.activation(
                                  usq[:], xt, AF.Square, accum_out=s_sq),
                              wait=aw)
                    b.op("vector",
                         lambda e, xt=xt, s_sum=s_sum: e.reduce_sum(
                             s_sum, xt, AX.X), wait=aw)
                    b.op("vector", lambda e, s_m=s_m, s_sum=s_sum:
                         e.tensor_scalar_mul(s_m, s_sum, 1.0 / DM))
                    b.op("vector", lambda e, s_v=s_v, s_sq=s_sq:
                         e.tensor_scalar_mul(s_v, s_sq, 1.0 / DM),
                         wait=[a1])
                    b.op("vector", lambda e, s_m2=s_m2, s_m=s_m:
                         e.tensor_mul(s_m2, s_m, s_m))
                    d2 = b.op("vector", lambda e, s_v=s_v, s_m2=s_m2:
                              e.tensor_sub(s_v, s_v, s_m2))
                    a2 = b.op("scalar", lambda e, s_v=s_v, s_r0=s_r0:
                              e.activation(s_r0, s_v, AF.Sqrt,
                                           bias=self.constv[:, 0:1]),
                              wait=[d2])
                    b.op("vector", lambda e, s_r=s_r, s_r0=s_r0:
                         e.reciprocal(s_r, s_r0), wait=[a2])
                    d4 = b.op("vector", lambda e, s_nb=s_nb, s_m=s_m, s_r=s_r:
                              e.tensor_scalar(s_nb, s_m, s_r, -1.0, ALU.mult,
                                              ALU.mult))
                    uw = [d4]
                    key = (grp, t - 2) if t >= 2 else (grp - 1, t + 2)
                    if key in tp_done:
                        uw.append(tp_done[key])
                    du = b.op("vector",
                              lambda e, xt=xt, t=t, s_r=s_r, s_nb=s_nb:
                              e.tensor_scalar(ub[t % 2][:], xt, s_r, s_nb,
                                              ALU.mult, ALU.add),
                              wait=uw)
                    tw = [du]
                    if key in ufm_evac:
                        tw.append(ufm_evac[key])
                    tp = None
                    for ic in range(NIC_DM):
                        tp = b.op("tensor",
                                  lambda e, t=t, ic=ic: e.matmul(
                                      pT[t % 2][:, ic * P:(ic + 1) * P],
                                      ub[t % 2][:, ic * P:(ic + 1) * P],
                                      identb[:], start=True, stop=True),
                                  wait=tw if ic == 0 else [],
                                  inc=pe if ic == NIC_DM - 1 else None)
                    tp_done[(grp, t)] = tp
                    ew = [tp]
                    if grp > 0 and t == 0 and last_w2_read_ufm:
                        ew.append(last_w2_read_ufm)
                    ufm_out = u_fm[:].rearrange(
                        "p (ic g) -> p ic g", ic=NIC_DM)[:, :, t * P:(t + 1) * P]
                    pt_src = pT[t % 2][:].rearrange(
                        "p (ic q) -> p ic q", ic=NIC_DM)
                    ufm_evac[(grp, t)] = b.op(
                        "vector",
                        lambda e, ufm_out=ufm_out, pt_src=pt_src:
                        e.tensor_copy(ufm_out, pt_src),
                        wait=ew)
                ufm_ready = ufm_evac[(grp, 3)]
                if self.tap == ("ufm", grp):
                    self._tapd(u_fm, NIC_DM * GT, [ufm_ready])
                    return

                y1_mult = {}
                silu_evac = {}
                for oc in range(NOC_DH):
                    sl1, w1g = r_w1.load(
                        bass.AP(self.c_w1, oc * P * DM, [[DM, P], [1, DM]]))
                    sl2, w2g = r_w2.load(
                        bass.AP(self.c_w2, oc * P * DM, [[DM, P], [1, DM]]))
                    psA = pW[oc % 2][:, 0:GT]
                    psB = pW[oc % 2][:, GT:2 * GT]
                    mw = [w1g, ufm_ready]
                    if oc >= 2:
                        mw.append(silu_evac[oc - 2][0])
                    elif res_last:
                        mw.append(res_last)
                    w1_tok = None
                    for ic in range(NIC_DM):
                        w1_tok = b.op(
                            "tensor",
                            lambda e, ic=ic, sl1=sl1, psA=psA: e.matmul(
                                psA, r_w1.bufs[sl1][:, ic * P:(ic + 1) * P],
                                u_fm[:, ic * GT:(ic + 1) * GT],
                                start=(ic == 0), stop=(ic == NIC_DM - 1)),
                            wait=mw if ic == 0 else [],
                            inc=f_w1 if ic == NIC_DM - 1 else None)
                    r_w1.set_free(sl1, w1_tok)
                    mw = [w2g]
                    if oc >= 2:
                        mw.append(silu_evac[oc - 2][1])
                    w2_tok = None
                    for ic in range(NIC_DM):
                        w2_tok = b.op(
                            "tensor",
                            lambda e, ic=ic, sl2=sl2, psB=psB: e.matmul(
                                psB, r_w2.bufs[sl2][:, ic * P:(ic + 1) * P],
                                u_fm[:, ic * GT:(ic + 1) * GT],
                                start=(ic == 0), stop=(ic == NIC_DM - 1)),
                            wait=mw if ic == 0 else [],
                            inc=f_w2 if ic == NIC_DM - 1 else None)
                    r_w2.set_free(sl2, w2_tok)
                    last_w2_read_ufm = w2_tok
                    ew = [w1_tok]
                    if oc >= 2:
                        ew.append(y1_mult[oc - 2])
                    elif t1b_free[oc % 2]:
                        ew.append(t1b_free[oc % 2])
                    e1 = b.op("scalar",
                              lambda e, oc=oc, psA=psA: e.activation(
                                  t1b[oc % 2][:], psA, AF.Silu,
                                  scale=s1s[:, oc:oc + 1]),
                              wait=ew)
                    e2 = b.op("scalar",
                              lambda e, oc=oc, psB=psB: e.activation(
                                  t2b[oc % 2][:], psB, AF.Copy,
                                  scale=s2s[:, oc:oc + 1]),
                              wait=[w2_tok])
                    silu_evac[oc] = (e1, e2)
                    mw = [e2]
                    if oc == 0 and last_w3_read_y3:
                        mw.append(last_w3_read_y3)
                    y1_mult[oc] = b.op(
                        "vector",
                        lambda e, oc=oc: e.tensor_mul(
                            y1[:, oc * GT:(oc + 1) * GT], t1b[oc % 2][:],
                            t2b[oc % 2][:]),
                        wait=mw)
                t1b_free = [y1_mult[NOC_DH - 2], y1_mult[NOC_DH - 1]]
                y1_done = y1_mult[NOC_DH - 1]
                if self.tap == ("y1", grp):
                    self._tapd(y1, NOC_DH * GT, [y1_done])
                    return

                prev_done = y1_done
                h_in = [y1, y2]
                caches = [self.c_h1, self.c_h2]
                scales = [hs1s, hs2s]
                outs = [y2, y1]
                for stage in range(2):
                    gelu_evac = {}
                    for oc in range(NOC_DH):
                        slh, hg = r_h.load(
                            bass.AP(caches[stage], oc * P * DH,
                                    [[DH, P], [1, DH]]))
                        ps = pW[oc % 2][:, 0:GT]
                        mw = [hg, prev_done]
                        if oc >= 2:
                            mw.append(gelu_evac[oc - 2])
                        elif stage == 0:
                            mw.append(silu_evac[NOC_DH - 2 + oc][0])
                        h_tok = None
                        for ic in range(NOC_DH):
                            h_tok = b.op(
                                "tensor",
                                lambda e, ic=ic, slh=slh, ps=ps, stage=stage:
                                e.matmul(
                                    ps, r_h.bufs[slh][:, ic * P:(ic + 1) * P],
                                    h_in[stage][:, ic * GT:(ic + 1) * GT],
                                    start=(ic == 0), stop=(ic == NOC_DH - 1)),
                                wait=mw if ic == 0 else [],
                                inc=f_h if ic == NOC_DH - 1 else None)
                        r_h.set_free(slh, h_tok)
                        gelu_evac[oc] = b.op(
                            "scalar",
                            lambda e, oc=oc, ps=ps, stage=stage: e.activation(
                                outs[stage][:, oc * GT:(oc + 1) * GT], ps,
                                AF.Gelu, scale=scales[stage][:, oc:oc + 1]),
                            wait=[h_tok])
                    prev_done = gelu_evac[NOC_DH - 1]
                    if self.tap == ("y2" if stage == 0 else "y3", grp):
                        self._tapd(outs[stage], NOC_DH * GT, [prev_done])
                        return
                y3_done = prev_done

                res_adds = []
                for hp in range(2):
                    w3_tok = None
                    for ic in range(NOC_DH):
                        sl3, w3g = r_w3.load(
                            bass.AP(self.c_w3, ic * P * DM, [[DM, P], [1, DM]]))
                        mw = [w3g]
                        if ic == 0:
                            mw.append(y3_done)
                            if hp == 1:
                                mw.append(res_adds[-1])
                        for t_loc in range(2):
                            t = hp * 2 + t_loc
                            for half in range(2):
                                ps = pW[t_loc][:, half * GT:(half + 1) * GT]
                                last_mm = (t_loc == 1 and half == 1)
                                w3_tok2 = b.op(
                                    "tensor",
                                    lambda e, ic=ic, sl3=sl3, t=t, ps=ps,
                                    half=half: e.matmul(
                                        ps,
                                        y1[:, ic * GT + t * P:
                                           ic * GT + (t + 1) * P],
                                        r_w3.bufs[sl3][:,
                                                       half * GT:
                                                       (half + 1) * GT],
                                        start=(ic == 0),
                                        stop=(ic == NOC_DH - 1)),
                                    wait=mw if (t_loc == 0 and half == 0)
                                    else [],
                                    inc=f_w3 if last_mm else None)
                                if last_mm:
                                    w3_tok = w3_tok2
                        r_w3.set_free(sl3, w3_tok)
                    w3_done = w3_tok
                    last_w3_read_y3 = w3_done
                    for t_loc in range(2):
                        t = hp * 2 + t_loc
                        rw = [w3_done]
                        if grp > 0 and hp == 0 and t_loc == 0:
                            rw.append(out_tok[grp - 1])
                        r = b.op("vector",
                                 lambda e, t=t, t_loc=t_loc, slot=slot:
                                 e.tensor_add(
                                     outg[:, t * DM:(t + 1) * DM],
                                     pW[t_loc][:],
                                     xg[slot][:, t * DM:(t + 1) * DM]),
                                 wait=rw)
                        res_adds.append(r)
                res_last = res_adds[-1]
                res_by_grp[grp] = res_last
                out_tok[grp] = b.dma("gpsimd",
                                     rows_dst.rearrange("(t p) d -> p t d", p=P),
                                     outg[:].rearrange("p (t d) -> p t d", t=4),
                                     "fxo", wait=[res_last])

            b.wait("gpsimd", [out_tok[NGRP - 1]])

    def emit_attn(self, stack, variant, src, dst):
        b, nc = self.b, self.nc
        is_a = variant == "a"
        tg = "a" if is_a else "t"

        with ExitStack() as ph:
            def sbuf(name, shape, dtype):
                return ph.enter_context(nc.sbuf_tensor(tg + name, shape, dtype))

            qT = sbuf("qT", [P, 8 * SEQ], BF16)
            kT = sbuf("kT", [P, 8 * SEQ], BF16)
            vsb = sbuf("v", [P, NQT * DM], BF16)

            relk_ready = None
            if not is_a:
                with nc.sbuf_tensor(tg + "posT", [P, 2 * NPOS], BF16) as posb, \
                     nc.sbuf_tensor(tg + "wrel", [P, 2 * DM], BF16) as wrelb, \
                     nc.sbuf_tensor(tg + "rko", [P, NPOS], BF16) as rko, \
                     nc.psum_tensor(tg + "rkp", [P, 512], F32) as rkp:
                    t1 = b.dma("gpsimd", posb[:, 0:NPOS],
                               self.posT.ap()[0:P, :], "blpos")
                    t2 = b.dma("gpsimd", posb[0:64, NPOS:2 * NPOS],
                               self.posT.ap()[P:RFEAT, :], "blpos2")
                    t3 = b.dma("gpsimd", wrelb[:, 0:DM],
                               self.wbrel.ap()[0:P, :], "blrel")
                    t3b = b.dma("gpsimd", wrelb[0:64, DM:2 * DM],
                                self.wbrel.ap()[P:RFEAT, :], "blrel2")
                    rk_dma = None
                    rk_prev = None
                    for h in range(NH_B):
                        ev_prev = None
                        for c in range(8):
                            n = 512 if c < 7 else 511
                            mm = None
                            for kc in range(2):
                                rows = P if kc == 0 else 64
                                mm = b.op(
                                    "tensor",
                                    lambda e, h=h, c=c, kc=kc, rows=rows, n=n:
                                    e.matmul(
                                        rkp[:, 0:n],
                                        wrelb[0:rows,
                                              kc * DM + h * P:
                                              kc * DM + (h + 1) * P],
                                        posb[0:rows,
                                             kc * NPOS + c * 512:
                                             kc * NPOS + c * 512 + n],
                                        start=(kc == 0), stop=(kc == 1)),
                                    wait=([t for t in
                                           (t1, t2, t3, t3b, ev_prev, rk_prev)
                                           if t] if kc == 0 else []),
                                    inc="brpe" if kc == 1 else None)
                            ev_prev = b.op(
                                "vector",
                                lambda e, c=c, n=n: e.tensor_copy(
                                    rko[:, c * 512:c * 512 + n], rkp[:, 0:n]),
                                wait=[mm, rk_prev] if rk_prev else [mm])
                        rk_dma = b.dma(
                            "gpsimd",
                            bass.AP(self.c_relkT, h * P * NPOS,
                                    [[NPOS, P], [1, NPOS]]),
                            rko[:], "blrk", wait=[ev_prev])
                        rk_prev = rk_dma
                    relk_ready = rk_dma
                    b.wait("gpsimd", [("blrk", b.counts["blrk"])])

            for s in range(NSEQ):
                self._attn_seq(b, nc, variant, src, dst, s,
                               qT, kT, vsb, relk_ready)
                if self._tap_hit:
                    return

    def _attn_seq(self, b, nc, variant, src, dst, s, qT, kT, vsb, relk_ready):
        identb = self.identb
        is_a = variant == "a"
        tg = ("a" if is_a else "t") + str(s)
        NH = NH_A if is_a else NH_B
        HD = HD_A if is_a else HD_B
        NCH = 8
        row0 = s * SEQ
        niter = NH * NQT
        wq_in = self.waq if is_a else self.wbq
        wk_in = self.wak if is_a else self.wbk
        wv_in = self.wav if is_a else self.wbv
        wo_in = self.wao if is_a else self.wbo

        pe = "ape"
        f_q = "afq"
        f_o = "afo"

        with ExitStack() as p1:
            def sbuf1(name, shape, dtype):
                return p1.enter_context(nc.sbuf_tensor(tg + name, shape, dtype))

            wqs = sbuf1("wq", [P, NCH * DM], BF16)
            wks = sbuf1("wk", [P, NCH * DM], BF16)
            wvs = sbuf1("wv", [P, NCH * DM], BF16)
            xb = [sbuf1("xb0", [P, DM], F32), sbuf1("xb1", [P, DM], F32)]
            usq = sbuf1("usq", [P, DM], BF16)
            ub = [sbuf1("ub0", [P, DM], BF16), sbuf1("ub1", [P, DM], BF16)]
            u_fm = sbuf1("ufm", [P, NCH * GT], BF16)
            stats = sbuf1("stats", [P, 8], F32)
            psQK = [p1.enter_context(nc.psum_tensor(tg + "psQ0", [P, DM], F32)),
                    p1.enter_context(nc.psum_tensor(tg + "psQ1", [P, DM], F32))]
            psV = [p1.enter_context(nc.psum_tensor(tg + "psV0", [P, DM], F32)),
                   p1.enter_context(nc.psum_tensor(tg + "psV1", [P, DM], F32))]

            wtoks = []
            for dsrc, dstb, semn in ((wq_in, wqs, "alwq"),
                                     (wk_in, wks, "alwk"),
                                     (wv_in, wvs, "alwv")):
                ap = bass.AP(dsrc, 0, [[DM, P], [P * DM, NCH], [1, DM]])
                wtoks.append(b.dma("gpsimd", dstb[:].rearrange(
                    "p (ic o) -> p ic o", ic=NCH), ap, semn))
            wq_tok, wk_tok, wv_tok = wtoks

            x_sem = ["axl0", "axl1"]
            ln_last = [None, None]
            tp_done = {}
            ufm_evac = {}
            qk_evac = {}
            v_evac = {}
            last_qkv_read_ufm = None
            for g in range(4):
                for t in range(4):
                    ti = g * 4 + t
                    slot = ti % 2
                    xt = xb[slot][:]
                    w = [ln_last[slot]] if ln_last[slot] else []
                    xld = b.dma("gpsimd", xt,
                                src.ap()[row0 + ti * P:row0 + (ti + 1) * P, :],
                                x_sem[slot], wait=w)
                    s_sum = stats[:, 0:1]
                    s_sq = stats[:, 1:2]
                    s_m = stats[:, 2:3]
                    s_v = stats[:, 3:4]
                    s_r0 = stats[:, 4:5]
                    s_r = stats[:, 5:6]
                    s_m2 = stats[:, 6:7]
                    s_nb = stats[:, 7:8]
                    aw = [xld]
                    a1 = b.op("scalar",
                              lambda e, xt=xt, s_sq=s_sq: e.activation(
                                  usq[:], xt, AF.Square, accum_out=s_sq),
                              wait=aw)
                    b.op("vector",
                         lambda e, xt=xt, s_sum=s_sum: e.reduce_sum(
                             s_sum, xt, AX.X), wait=aw)
                    b.op("vector", lambda e, s_m=s_m, s_sum=s_sum:
                         e.tensor_scalar_mul(s_m, s_sum, 1.0 / DM))
                    b.op("vector", lambda e, s_v=s_v, s_sq=s_sq:
                         e.tensor_scalar_mul(s_v, s_sq, 1.0 / DM),
                         wait=[a1])
                    b.op("vector", lambda e, s_m2=s_m2, s_m=s_m:
                         e.tensor_mul(s_m2, s_m, s_m))
                    d2 = b.op("vector", lambda e, s_v=s_v, s_m2=s_m2:
                              e.tensor_sub(s_v, s_v, s_m2))
                    a2 = b.op("scalar", lambda e, s_v=s_v, s_r0=s_r0:
                              e.activation(s_r0, s_v, AF.Sqrt,
                                           bias=self.constv[:, 0:1]),
                              wait=[d2])
                    b.op("vector", lambda e, s_r=s_r, s_r0=s_r0:
                         e.reciprocal(s_r, s_r0), wait=[a2])
                    d4 = b.op("vector", lambda e, s_nb=s_nb, s_m=s_m, s_r=s_r:
                              e.tensor_scalar(s_nb, s_m, s_r, -1.0, ALU.mult,
                                              ALU.mult))
                    uw = [d4]
                    if (ti - 2) >= 0 and (ti - 2) in tp_done:
                        uw.append(tp_done[ti - 2])
                    du = b.op("vector",
                              lambda e, xt=xt, slot=slot, s_r=s_r, s_nb=s_nb:
                              e.tensor_scalar(ub[slot][:], xt, s_r, s_nb,
                                              ALU.mult, ALU.add),
                              wait=uw)
                    ln_last[slot] = du
                    tw = [du]
                    if (ti - 2) in ufm_evac:
                        tw.append(ufm_evac[ti - 2])
                    if t == 0 and g > 0:
                        tw.append(qk_evac[(1, NCH - 2, g - 1)])
                        tw.append(qk_evac[(1, NCH - 1, g - 1)])
                    tp = None
                    for ic in range(NCH):
                        tp = b.op("tensor",
                                  lambda e, t=t, ic=ic, slot=slot: e.matmul(
                                      psQK[t % 2][:, ic * P:(ic + 1) * P],
                                      ub[slot][:, ic * P:(ic + 1) * P],
                                      identb[:], start=True, stop=True),
                                  wait=tw if ic == 0 else [],
                                  inc=pe if ic == NCH - 1 else None)
                    tp_done[ti] = tp
                    ufm_out = u_fm[:].rearrange(
                        "p (ic gg) -> p ic gg", ic=NCH)[:, :, t * P:(t + 1) * P]
                    pt_src = psQK[t % 2][:].rearrange(
                        "p (a q) -> p a q", a=NCH)
                    ufm_evac[ti] = b.op(
                        "vector",
                        lambda e, ufm_out=ufm_out, pt_src=pt_src:
                        e.tensor_copy(ufm_out, pt_src),
                        wait=[tp])
                ufm_ready = ufm_evac[g * 4 + 3]

                for which, wbuf, wtok, dstT, scale_q in (
                        (0, wqs, wq_tok, qT, True),
                        (1, wks, wk_tok, kT, False)):
                    for oc in range(NCH):
                        psp = psQK[oc % 2][:, which * GT:(which + 1) * GT]
                        mw = [ufm_ready, wtok]
                        pk = ((which, oc - 2, g) if oc >= 2
                              else (1, NCH - 2 + oc, g - 1))
                        if pk in qk_evac:
                            mw.append(qk_evac[pk])
                        mtok = None
                        for ic in range(NCH):
                            mtok = b.op(
                                "tensor",
                                lambda e, oc=oc, ic=ic, wbuf=wbuf, psp=psp:
                                e.matmul(
                                    psp,
                                    wbuf[:, ic * DM + oc * P:
                                         ic * DM + (oc + 1) * P],
                                    u_fm[:, ic * GT:(ic + 1) * GT],
                                    start=(ic == 0), stop=(ic == NCH - 1)),
                                wait=mw if ic == 0 else [],
                                inc=f_q if ic == NCH - 1 else None)
                        qcol = 1 if is_a else 2
                        sc = self.constv[:, qcol:qcol + 1] if scale_q else 1.0
                        qk_evac[(which, oc, g)] = b.op(
                            "scalar",
                            lambda e, oc=oc, g=g, psp=psp, sc=sc, dstT=dstT:
                            e.activation(
                                dstT[:, oc * SEQ + g * GT:
                                     oc * SEQ + (g + 1) * GT],
                                psp, AF.Copy, scale=sc),
                            wait=[mtok])
                for t in range(4):
                    ti = g * 4 + t
                    psp = psV[t % 2]
                    mw = [ufm_ready, wv_tok]
                    pk = ti - 2
                    if pk in v_evac:
                        mw.append(v_evac[pk])
                    mtok = None
                    for ic in range(NCH):
                        for half in range(2):
                            mtok = b.op(
                                "tensor",
                                lambda e, t=t, ic=ic, half=half, psp=psp:
                                e.matmul(
                                    psp[:, half * GT:(half + 1) * GT],
                                    u_fm[:, ic * GT + t * P:
                                         ic * GT + (t + 1) * P],
                                    wvs[:, ic * DM + half * GT:
                                        ic * DM + (half + 1) * GT],
                                    start=(ic == 0), stop=(ic == NCH - 1)),
                                wait=mw if (ic == 0 and half == 0) else [],
                                inc=f_q if (ic == NCH - 1 and half == 1)
                                else None)
                    last_qkv_read_ufm = mtok
                    v_evac[ti] = b.op(
                        "vector",
                        lambda e, ti=ti, psp=psp: e.tensor_copy(
                            vsb[:, ti * DM:(ti + 1) * DM], psp[:]),
                        wait=[mtok])
            qkv_ready = [qk_evac[(0, NCH - 1, 3)], qk_evac[(1, NCH - 1, 3)],
                         v_evac[15]]
        if self.tap == ("qkv" + variant, s):
            self._tapd(qT, NCH * SEQ, qkv_ready)
            self._tapd(kT, NCH * SEQ, qkv_ready, row0=16)
            return

        with ExitStack() as p23:
            afm = p23.enter_context(
                nc.sbuf_tensor(tg + "afm", [P, NCH * SEQ], BF16))
            attn_done = self._attn_scores(
                b, nc, variant, s, qT, kT, vsb, afm, qkv_ready, relk_ready)
            if self._tap_hit:
                return

            with ExitStack() as p3:
                def sbuf3(name, shape, dtype):
                    return p3.enter_context(
                        nc.sbuf_tensor(tg + name, shape, dtype))

                wos = sbuf3("wo", [P, NCH * DM], BF16)
                xres = [sbuf3("xr0", [P, DM], F32), sbuf3("xr1", [P, DM], F32)]
                outr = [sbuf3("or0", [P, DM], F32), sbuf3("or1", [P, DM], F32)]
                psW = [p3.enter_context(
                           nc.psum_tensor(tg + "psW0", [P, DM], F32)),
                       p3.enter_context(
                           nc.psum_tensor(tg + "psW1", [P, DM], F32))]
                wo_tok = b.dma("gpsimd", wos[:].rearrange(
                    "p (ic o) -> p ic o", ic=NCH),
                    bass.AP(wo_in, 0, [[DM, P], [P * DM, NCH], [1, DM]]),
                    "alwo", wait=[attn_done])
                xr_sem = ["axr0", "axr1"]
                add_tok = {}
                st_tok = {}
                for qt in range(NQT):
                    slot = qt % 2
                    w = [add_tok[qt - 2]] if qt >= 2 else [attn_done]
                    xld = b.dma("gpsimd", xres[slot][:],
                                src.ap()[row0 + qt * P:row0 + (qt + 1) * P, :],
                                xr_sem[slot], wait=w)
                    mw = [attn_done, wo_tok]
                    if qt >= 2:
                        mw.append(add_tok[qt - 2])
                    mtok = None
                    for ic in range(NCH):
                        for half in range(2):
                            mtok = b.op(
                                "tensor",
                                lambda e, qt=qt, ic=ic, half=half, slot=slot:
                                e.matmul(
                                    psW[slot][:, half * GT:(half + 1) * GT],
                                    afm[:, ic * SEQ + qt * P:
                                        ic * SEQ + (qt + 1) * P],
                                    wos[:, ic * DM + half * GT:
                                        ic * DM + (half + 1) * GT],
                                    start=(ic == 0), stop=(ic == NCH - 1)),
                                wait=mw if (ic == 0 and half == 0) else [],
                                inc=f_o if (ic == NCH - 1 and half == 1)
                                else None)
                    aw = [mtok, xld]
                    if qt >= 2 and (qt - 2) in st_tok:
                        aw.append(st_tok[qt - 2])
                    if is_a:
                        add_tok[qt] = b.op(
                            "vector",
                            lambda e, slot=slot: e.tensor_add(
                                outr[slot][:], psW[slot][:], xres[slot][:]),
                            wait=aw)
                    else:
                        t0 = b.op("vector",
                                  lambda e, slot=slot: e.tensor_add(
                                      outr[slot][:], psW[slot][:],
                                      self.boutbc[:]),
                                  wait=aw)
                        add_tok[qt] = b.op(
                            "vector",
                            lambda e, slot=slot: e.tensor_add(
                                outr[slot][:], outr[slot][:], xres[slot][:]),
                            wait=[t0])
                    st_tok[qt] = b.dma(
                        "gpsimd",
                        dst.ap()[row0 + qt * P:row0 + (qt + 1) * P, :],
                        outr[slot][:], "ast" + str(slot),
                        wait=[add_tok[qt]])
                b.wait("gpsimd", [st_tok[NQT - 1], st_tok[NQT - 2]])

    def _attn_scores(self, b, nc, variant, s, qT, kT, vsb, afm, qkv_ready,
                     relk_ready):
        identb = self.identb
        is_a = variant == "a"
        tg = ("a" if is_a else "t") + str(s) + "s"
        NH = NH_A if is_a else NH_B
        niter = NH * NQT
        f_s = "afs"
        f_t = "aft"
        f_v = "afv"
        f_m = "afm_"

        NPB = 3 if is_a else 2
        with ExitStack() as p2:
            def sbuf2(name, shape, dtype):
                return p2.enter_context(nc.sbuf_tensor(tg + name, shape, dtype))

            St_sb = [sbuf2("St0", [P, SEQ], BF16), sbuf2("St1", [P, SEQ], BF16)]
            P_sb = [sbuf2("P%d" % k, [P, SEQ], BF16) for k in range(NPB)]
            PT_sb = [sbuf2("PT%d" % k, [P, SEQ], BF16) for k in range(NPB)]
            sst = [sbuf2("ss%d" % k, [P, 4], F32) for k in range(NPB)]
            ps_S = p2.enter_context(nc.psum_tensor(tg + "psS", [P, SEQ], F32))
            ps_Tb = p2.enter_context(nc.psum_tensor(tg + "psT", [P, SEQ], BF16))
            ps_PV = p2.enter_context(nc.psum_tensor(tg + "psPV", [P, P], F32))
            if not is_a:
                ps_Mt = p2.enter_context(
                    nc.psum_tensor(tg + "psMt", [P, P], F32))
                M_sb = [sbuf2("M0", [P, RWIN], BF16),
                        sbuf2("M1", [P, RWIN], BF16)]
                RS_sb = [sbuf2("R0", [P, SEQ], BF16),
                         sbuf2("R1", [P, SEQ], BF16)]
                qa_sb = [sbuf2("qa0", [P, P], BF16), sbuf2("qa1", [P, P], BF16)]
                qb_sb = [sbuf2("qb0", [P, P], BF16), sbuf2("qb1", [P, P], BF16)]
                relk = [sbuf2("rk0", [P, NPOS], BF16),
                        sbuf2("rk1", [P, NPOS], BF16)]
                relk_sem = ["ark0", "ark1"]
                relk_tok = [None, None]
                mw_sem = ["amw0", "amw1"]
                ac_sem = ["aac0", "aac1"]
                mring_free = {}
                msb_free = [None, None]
                rs_tok = {}
                mev_tok = {}
                qbias_free = {0: [None, None], 1: [None, None]}

            s_tok = {}
            ev_tok = {}
            exp_tok = {}
            norm_tok = {}
            tp_tok = {}
            ptev_tok = {}
            pv_tok = {}
            pvev_tok = {}

            def idx_hq(i):
                return i // NQT, i % NQT

            def emit_qbias(i, which):
                h, qt = idx_hq(i)
                buf = (qa_sb if which == 0 else qb_sb)[i % 2]
                bias = (self.rebs if which == 0 else self.rpbs)[:, h:h + 1]
                gate = list(qkv_ready)
                if qbias_free[which][i % 2]:
                    gate.append(qbias_free[which][i % 2])
                return b.op("vector",
                            lambda e, h=h, qt=qt, buf=buf, bias=bias:
                            e.tensor_scalar_add(
                                buf[:], qT[:, h * SEQ + qt * P:
                                           h * SEQ + (qt + 1) * P], bias),
                            wait=gate)

            def emit_S(i):
                h, qt = idx_hq(i)
                is_a_l = is_a
                if is_a:
                    ch, rb = h // 2, 64 * (h % 2)
                    lhs_fn = (lambda ch=ch, rb=rb, qt=qt:
                              qT[rb:rb + 64, ch * SEQ + qt * P:
                                 ch * SEQ + (qt + 1) * P])
                    rhs_fn = (lambda c, ch=ch, rb=rb: kT[
                        rb:rb + 64,
                        ch * SEQ + c * 512:ch * SEQ + (c + 1) * 512])
                    qa_tok = None
                else:
                    qa_tok = emit_qbias(i, 0)
                    lhs_fn = lambda i=i: qa_sb[i % 2][:]
                    rhs_fn = (lambda c, h=h: kT[
                        :, h * SEQ + c * 512:h * SEQ + (c + 1) * 512])
                toks = []
                for c in range(4):
                    w = []
                    if c == 0:
                        w = list(qkv_ready)
                        if qa_tok:
                            w.append(qa_tok)
                        if not is_a and i + 1 in mev_tok:
                            w += list(mev_tok[i + 1][:2])
                        elif i >= 1:
                            w += list(ev_tok[i - 1])
                    toks.append(b.op(
                        "tensor",
                        lambda e, c=c, lhs_fn=lhs_fn, rhs_fn=rhs_fn:
                        e.matmul(ps_S[:, c * 512:(c + 1) * 512],
                                 lhs_fn(), rhs_fn(c),
                                 start=True, stop=is_a_l),
                        wait=w, inc=(f_s if (c == 3 and is_a_l) else None)))
                if not is_a:
                    qbias_free[0][i % 2] = toks[-1]
                    last = None
                    for c in range(4):
                        w = [rs_tok[i]] if c == 0 else []
                        last = b.op(
                            "tensor",
                            lambda e, c=c, i=i: e.matmul(
                                ps_S[:, c * 512:(c + 1) * 512],
                                identb[:],
                                RS_sb[i % 2][:, c * 512:(c + 1) * 512],
                                start=False, stop=True),
                            wait=w, inc=f_s if c == 3 else None)
                    s_tok[i] = last
                else:
                    s_tok[i] = toks[-1]

            def emit_M(j):
                h, qt = idx_hq(j)
                r_lo = (SEQ - P) - qt * P
                qb_tok = emit_qbias(j, 1)
                slot_rk = h % 2
                mslot = j % 2
                dslot = j % self.NMRING
                mtok = None
                for c in range(4):
                    w = []
                    if c == 0:
                        w = [qb_tok, relk_tok[slot_rk]]
                        if (j - 2) >= 0 and (j - 2) in ev_tok:
                            w += list(ev_tok[j - 2])
                        elif j == 1:
                            w += [mev_tok[0][0], mev_tok[0][1]]
                    mtok = b.op(
                        "tensor",
                        lambda e, c=c, mslot=mslot, r_lo=r_lo,
                        slot_rk=slot_rk: e.matmul(
                            ps_S[:, c * 512:(c + 1) * 512],
                            qb_sb[mslot][:],
                            relk[slot_rk][:, r_lo + c * 512:
                                          r_lo + (c + 1) * 512],
                            start=True, stop=True),
                        wait=w, inc=f_m if c == 3 else None)
                mt_tok = b.op(
                    "tensor",
                    lambda e, mslot=mslot, r_lo=r_lo, slot_rk=slot_rk:
                    e.matmul(
                        ps_Mt[:, 0:127],
                        qb_sb[mslot][:],
                        relk[slot_rk][:, r_lo + 2048:r_lo + 2175],
                        start=True, stop=True),
                    wait=[mev_tok[j - 1][2]] if j >= 1 else [],
                    inc=f_m)
                qbias_free[1][j % 2] = mt_tok
                w1_ = [mtok]
                if msb_free[mslot]:
                    w1_.append(msb_free[mslot])
                me_d = b.op("vector",
                            lambda e, mslot=mslot: e.tensor_copy(
                                M_sb[mslot][:, 0:1024], ps_S[:, 0:1024]),
                            wait=w1_)
                me_a = b.op("scalar",
                            lambda e, mslot=mslot: e.activation(
                                M_sb[mslot][:, 1024:2048], ps_S[:, 1024:2048],
                                AF.Copy),
                            wait=[mtok] + ([msb_free[mslot]]
                                           if msb_free[mslot] else []))
                me_t = b.op("scalar",
                            lambda e, mslot=mslot: e.activation(
                                M_sb[mslot][:, 2048:2175], ps_Mt[:, 0:127],
                                AF.Copy),
                            wait=[mt_tok])
                mev_tok[j] = (me_d, me_a, me_t)
                wgate = [me_d, me_a, me_t]
                if dslot in mring_free:
                    wgate.append(mring_free[dslot])
                wr = b.dma("gpsimd",
                           bass.AP(self.m_ring, dslot * P * RWIN,
                                   [[RWIN, P], [1, RWIN]]),
                           M_sb[mslot][:], mw_sem[mslot], wait=wgate)
                msb_free[mslot] = wr
                return wr

            wr_tok = {}
            mx_tok = {}

            def emit_ev(j):
                par = j % 2
                w = [s_tok[j]]
                if j >= 2:
                    w.append(exp_tok[j - 2])
                ev = b.op("scalar",
                          lambda e, par=par: e.activation(
                              St_sb[par][:], ps_S[:], AF.Copy),
                          wait=w)
                ev_tok[j] = (ev,)

            def emit_mx(j):
                spar = j % 2
                mneg = sst[j % NPB][:, 0:1]
                w = list(ev_tok[j])
                mx_tok[j] = b.op(
                    "vector",
                    lambda e, spar=spar, mneg=mneg: e.reduce_max(
                        mneg, St_sb[spar][:], AX.X, negate=True),
                    wait=w)

            def emit_rd(j):
                dslot = j % self.NMRING
                rw = [wr_tok[j]]
                if (j - 2) >= 0:
                    rw.append(s_tok[j - 2])
                rd = b.dma("gpsimd", RS_sb[j % 2][:],
                           bass.AP(self.m_ring, dslot * P * RWIN + 127,
                                   [[RWIN - 1, P], [1, SEQ]]),
                           ac_sem[j % 2], wait=rw)
                rs_tok[j] = rd
                mring_free[dslot] = rd

            if not is_a:
                relk_tok[0] = b.dma(
                    "gpsimd", relk[0][:],
                    bass.AP(self.c_relkT, 0, [[NPOS, P], [1, NPOS]]),
                    relk_sem[0], wait=[relk_ready] + list(qkv_ready))
                wr_tok[0] = emit_M(0)
                if niter > 1:
                    wr_tok[1] = emit_M(1)
                emit_rd(0)
            emit_S(0)
            emit_ev(0)
            emit_mx(0)
            if not is_a and niter > 2:
                wr_tok[2] = emit_M(2)
            if not is_a and niter > 1:
                emit_rd(1)
            if niter > 1:
                emit_S(1)
            if not is_a and niter > 2:
                emit_rd(2)

            for i in range(niter + 2):
                par = i % NPB
                spar = i % 2
                if not is_a:
                    j4 = i + 4
                    if j4 < niter and j4 % NQT == 0:
                        hh = j4 // NQT
                        gate = []
                        j_last = (hh - 2) * NQT + NQT - 1
                        if j_last in mev_tok:
                            gate = [mev_tok[j_last][2]]
                        elif hh == 1:
                            gate = list(qkv_ready)
                        relk_tok[hh % 2] = b.dma(
                            "gpsimd", relk[hh % 2][:],
                            bass.AP(self.c_relkT, hh * P * NPOS,
                                    [[NPOS, P], [1, NPOS]]),
                            relk_sem[hh % 2], wait=gate)

                if i + 1 < niter:
                    emit_ev(i + 1)
                    emit_mx(i + 1)
                if not is_a and i + 3 < niter:
                    wr_tok[i + 3] = emit_M(i + 3)
                if i + 2 < niter:
                    emit_S(i + 2)
                if not is_a and i + 3 < niter:
                    emit_rd(i + 3)

                if i < niter:
                    h, qt = idx_hq(i)
                    ch = h // 2 if is_a else h
                    lsum = sst[par][:, 1:2]
                    rl = sst[par][:, 2:3]
                    mneg = sst[i % NPB][:, 0:1]
                    ew = [mx_tok[i]]
                    if i >= NPB:
                        ew.append(tp_tok[i - NPB])
                    ex = b.op("scalar",
                              lambda e, par=par, spar=spar, mneg=mneg,
                              lsum=lsum:
                              e.activation(P_sb[par][:], St_sb[spar][:],
                                           AF.Exp, bias=mneg,
                                           accum_out=lsum),
                              wait=ew)
                    exp_tok[i] = ex
                    rc = b.op("vector",
                              lambda e, rl=rl, lsum=lsum: e.reciprocal(
                                  rl, lsum),
                              wait=[ex])
                    nt = b.op("vector",
                              lambda e, par=par, rl=rl: e.tensor_scalar_mul(
                                  P_sb[par][:], P_sb[par][:], rl),
                              wait=[rc])
                    norm_tok[i] = nt

                if 2 <= i <= niter + 1:
                    hpp, qtpp = idx_hq(i - 2)
                    chpp = hpp // 2 if is_a else hpp
                    if is_a:
                        dest = afm[64 * (hpp % 2):64 * (hpp % 2) + 64,
                                   chpp * SEQ + qtpp * P:
                                   chpp * SEQ + (qtpp + 1) * P]
                        srcp = ps_PV[0:64, :]
                    else:
                        dest = afm[:, chpp * SEQ + qtpp * P:
                                   chpp * SEQ + (qtpp + 1) * P]
                        srcp = ps_PV[:, :]
                    pvev_tok[i - 2] = b.op(
                        "scalar",
                        lambda e, dest=dest, srcp=srcp: e.activation(
                            dest, srcp, AF.Copy),
                        wait=[pv_tok[i - 2]])

                if 1 <= i <= niter:
                    hp, qtp = idx_hq(i - 1)
                    chp = hp // 2 if is_a else hp
                    parp = (i - 1) % NPB
                    w = [tp_tok[i - 1]]
                    if (i - 1 - NPB) >= 0:
                        w.append(pv_tok[i - 1 - NPB])
                    ptev_tok[i - 1] = b.op(
                        "vector",
                        lambda e, parp=parp: e.tensor_copy(
                            PT_sb[parp][:], ps_Tb[:]),
                        wait=w)

                if i < niter:
                    ttok = None
                    for kb in range(NKB):
                        w = []
                        if kb == 0:
                            w = [norm_tok[i]]
                            if i >= 1:
                                w.append(ptev_tok[i - 1])
                        ttok = b.op(
                            "tensor",
                            lambda e, kb=kb, par=par: e.transpose(
                                ps_Tb[:, kb * P:(kb + 1) * P],
                                P_sb[par][:, kb * P:(kb + 1) * P],
                                identb[:]),
                            wait=w, inc=f_t if kb == NKB - 1 else None)
                    tp_tok[i] = ttok

                if 1 <= i <= niter:
                    ptok = None
                    for kb in range(NKB):
                        w = []
                        if kb == 0:
                            w = [ptev_tok[i - 1]]
                            if i >= 2:
                                w.append(pvev_tok[i - 2])
                        if is_a:
                            lhs = (lambda kb=kb, hp=hp:
                                   vsb[:, kb * DM + hp * HD_A:
                                       kb * DM + (hp + 1) * HD_A])
                            out = ps_PV[0:64, :]
                        else:
                            lhs = (lambda kb=kb, hp=hp:
                                   vsb[:, kb * DM + hp * HD_B:
                                       kb * DM + (hp + 1) * HD_B])
                            out = ps_PV[:, :]
                        ptok = b.op(
                            "tensor",
                            lambda e, kb=kb, lhs=lhs, out=out, parp=parp:
                            e.matmul(
                                out, lhs(),
                                PT_sb[parp][:, kb * P:(kb + 1) * P],
                                start=(kb == 0), stop=(kb == NKB - 1)),
                            wait=w, inc=f_v if kb == NKB - 1 else None)
                    pv_tok[i - 1] = ptok

            if self.tap == ("afm" + variant, s):
                self._tapd(afm, 8 * SEQ, [pvev_tok[niter - 1]])
                return None
            return pvev_tok[niter - 1]

def kernel(**inputs):
    ins = _prepare_inputs(inputs)
    prog = Prog()
    nc = prog.build()
    in_maps = []
    for c in range(8):
        m = dict(ins["common"])
        m["x"] = ins["x_shards"][c]
        in_maps.append(m)
    res = run_bass_kernel_spmd(nc, in_maps, core_ids=list(range(8)))
    out = np.concatenate(
        [np.asarray(r["y"]).reshape(NSEQ, SEQ, DM) for r in res.results], axis=0)
    return out.astype(np.float32)


def _prepare_inputs(d):
    x = np.asarray(d["L0_attn_a_hidden_states"], np.float32)
    x_shards = [np.ascontiguousarray(x[2 * c:2 * c + 2].reshape(TOK, DM))
                for c in range(8)]

    def T(a):
        return np.ascontiguousarray(np.asarray(a, np.float32).T)

    def lay_s(s):
        return np.ascontiguousarray(
            np.asarray(s, np.float32).reshape(NOC_DH, P).T)

    common = {
        "waqT": T(d["L0_attn_a_Wq"]),
        "wakT": T(d["L0_attn_a_Wk"]),
        "wavT": T(d["L0_attn_a_Wv"]),
        "waoT": T(d["L0_attn_a_Wo"]),
        "wbq": np.ascontiguousarray(np.asarray(d["L0_attn_b_w_qs"], np.float32)),
        "wbk": np.ascontiguousarray(np.asarray(d["L0_attn_b_w_ks"], np.float32)),
        "wbv": np.ascontiguousarray(np.asarray(d["L0_attn_b_w_vs"], np.float32)),
        "wbo": np.ascontiguousarray(np.asarray(d["L0_attn_b_w_out"], np.float32)),
        "wbrel": np.ascontiguousarray(
            np.asarray(d["L0_attn_b_w_rel_pos_k"], np.float32)),
        "wbreb": np.ascontiguousarray(
            np.asarray(d["L0_attn_b_rel_embed_bias"], np.float32).reshape(
                NH_B, HD_B)),
        "wbrpb": np.ascontiguousarray(
            np.asarray(d["L0_attn_b_rel_pos_bias"], np.float32).reshape(
                NH_B, HD_B)),
        "boutrow": np.ascontiguousarray(
            np.asarray(d["L0_attn_b_b_out"], np.float32).reshape(1, DM)),
        "posT": _positions_T(),
        "w1T": T(d["L0_ffn_w1_w"]),
        "w2T": T(d["L0_ffn_w2_w"]),
        "h1T": T(d["L0_ffn_h1_w"]),
        "h2T": T(d["L0_ffn_h2_w"]),
        "w3T": T(d["L0_ffn_w3_w"]),
        "s1l": lay_s(d["L0_ffn_w1_s"]),
        "s2l": lay_s(d["L0_ffn_w2_s"]),
        "hs1l": lay_s(d["L0_ffn_h1_s"]),
        "hs2l": lay_s(d["L0_ffn_h2_s"]),
        "s3row": np.ascontiguousarray(
            np.asarray(d["L0_ffn_w3_s"], np.float32).reshape(1, DM)),
    }
    return {"x_shards": x_shards, "common": common}


# version 13
# speedup vs baseline: 1.0851x; 1.0426x over previous
import math
from contextlib import ExitStack

import numpy as np
import concourse.bass as bass
import concourse.mybir as mybir
from concourse.bass_utils import run_bass_kernel_spmd

F32 = mybir.dt.float32
BF16 = mybir.dt.bfloat16
AF = mybir.ActivationFunctionType
ALU = mybir.AluOpType
AX = mybir.AxisListType

P = 128
DM = 1024
DH = 4096
SEQ = 2048
NSEQ = 2
TOK = SEQ * NSEQ
NT = TOK // P
GT = 512
NGRP = TOK // GT
NIC_DM = DM // P
NOC_DH = DH // P
NH_A = 16
HD_A = 64
NH_B = 8
HD_B = 128
RWIN = SEQ + P - 1
NPOS = 2 * SEQ - 1
RFEAT = 192
EPS = 1e-5
NQT = SEQ // P
NKB = SEQ // P

ALL_ENGINES = ("tensor", "vector", "scalar", "gpsimd", "sync")
AUTO_ENGINES = ("vector", "scalar", "gpsimd")


class Builder:
    def __init__(self, nc):
        self.nc = nc
        self.streams = {e: [] for e in ALL_ENGINES}
        self.sem_order = []
        self.counts = {}
        self.sems = {}
        self.watermark = {e: {} for e in ALL_ENGINES}

    def sem(self, name):
        if name not in self.counts:
            self.sem_order.append(name)
            self.counts[name] = 0
        return name

    def _filter(self, engine, wait):
        wl = []
        wm = self.watermark[engine]
        best = {}
        for s, v in wait:
            if v is None or v <= 0:
                continue
            best[s] = max(best.get(s, 0), v)
        for s, v in best.items():
            if wm.get(s, 0) >= v:
                continue
            wl.append((s, v))
            wm[s] = v
        return wl

    def op(self, engine, fn, wait=(), inc=None, auto=True):
        wl = self._filter(engine, wait)
        tok = None
        incs = []
        if engine in AUTO_ENGINES and auto:
            assert inc is None
            a = self.sem("au_" + engine)
            prev = self.counts[a]
            wm = self.watermark[engine]
            if prev > 0 and wm.get(a, 0) < prev:
                wl.append((a, prev))
                wm[a] = prev
            self.counts[a] += 1
            tok = (a, self.counts[a])
            incs = [(a, 1)]
        elif inc is not None:
            s = self.sem(inc)
            self.counts[s] += 1
            tok = (s, self.counts[s])
            incs = [(s, 1)]
        self.streams[engine].append((fn, tuple(wl), tuple(incs)))
        return tok

    def dma(self, queue, out_ap, in_ap, sem, wait=(), **kw):
        s = self.sem(sem)
        prev = self.counts[s]
        wl = self._filter(queue, list(wait) + ([(s, prev)] if prev else []))
        self.counts[s] += 16
        tok = (s, self.counts[s])
        self.streams[queue].append((
            lambda e, o=out_ap, i=in_ap, kw=kw: e.dma_start(o, i, **kw),
            tuple(wl), ((s, 16),)))
        return tok

    def barrier(self):
        s = self.sem("bar")
        base = self.counts[s]
        for e in ALL_ENGINES:
            w = []
            if e in AUTO_ENGINES:
                a = self.sem("au_" + e)
                if self.counts[a] > 0:
                    w.append((a, self.counts[a]))
            self.counts[s] += 1
            self.streams[e].append((lambda eng: eng.nop(), tuple(w),
                                    ((s, 1),)))
        target = base + len(ALL_ENGINES)
        for e in ALL_ENGINES:
            self.streams[e].append((None, ((s, target),), ()))
            self.watermark[e] = {s: target}
        return target

    def wait(self, engine, tokens):
        wl = self._filter(engine, tokens)
        if wl:
            self.streams[engine].append((None, tuple(wl), ()))

    def emit(self, stack):
        nc = self.nc
        for name in self.sem_order:
            self.sems[name] = stack.enter_context(nc.semaphore(name))
        block = stack.enter_context(nc.Block())

        def make(engine_name):
            def _(eng):
                for fn, wait, inc in self.streams[engine_name]:
                    for s, v in wait:
                        eng.wait_ge(self.sems[s], v)
                    if fn is None:
                        continue
                    inst = fn(eng)
                    for s, a in inc:
                        inst.then_inc(self.sems[s], a)
            return _

        block.tensor(make("tensor"))
        block.vector(make("vector"))
        block.scalar(make("scalar"))
        block.gpsimd(make("gpsimd"))
        block.sync(make("sync"))


class Ring:

    def __init__(self, b, name, bufs, queue="sync"):
        self.b = b
        self.bufs = bufs
        self.n = len(bufs)
        self.queue = queue
        self.semn = [b.sem(f"{name}{i}") for i in range(self.n)]
        self.free_gate = [None] * self.n
        self.u = 0

    def load(self, in_ap, extra_wait=()):
        slot = self.u % self.n
        w = list(extra_wait)
        if self.free_gate[slot] is not None:
            w.append(self.free_gate[slot])
        tok = self.b.dma(self.queue, self.bufs[slot][:], in_ap,
                         self.semn[slot], wait=w)
        self.u += 1
        return slot, tok

    def set_free(self, slot, gate):
        self.free_gate[slot] = gate


def _positions_T():
    from scipy.special import gammaln

    seq_len, feature_size = SEQ, RFEAT
    distances = np.arange(-seq_len + 1, seq_len, dtype=np.float64)
    n = feature_size // 6
    absd = np.abs(distances)[:, None]
    max_range = math.log(seq_len) / math.log(2.0)
    half_life = 2.0 ** np.linspace(3.0, max_range, n)
    feat_exp = np.exp(-math.log(2.0) / half_life[None, :] * absd)
    center_widths = 2.0 ** np.arange(1, n + 1, dtype=np.float64) - 1.0
    feat_cm = (center_widths[None, :] > absd).astype(np.float64)
    stddev = seq_len / (2.0 * n)
    start_mean = seq_len / n
    mean = np.linspace(start_mean, float(seq_len), n)[None, :]
    conc = (mean / stddev) ** 2
    rate = mean / stddev**2
    with np.errstate(divide="ignore", invalid="ignore"):
        xlogy = np.where(conc - 1.0 == 0.0, 0.0, (conc - 1.0) * np.log(absd))
    log_unnorm = xlogy - rate * absd
    log_norm = gammaln(conc) - conc * np.log(rate)
    probs = np.exp(log_unnorm - log_norm) + 1e-08
    feat_gamma = probs / np.max(probs, axis=-1, keepdims=True)
    emb = np.concatenate([feat_exp, feat_cm, feat_gamma], axis=-1)
    emb = np.concatenate([emb, np.sign(distances)[:, None] * emb], axis=-1)
    return np.ascontiguousarray(emb.T.astype(np.float32))


class Prog:
    def __init__(self, phases=("attn_a", "ffn1", "attn_b", "ffn2")):
        self.phases = phases
        self.tap = None
        self.tap_i = 0
        self._tap_hit = False
        nc = self.nc = bass.Bass("TRN2", target_bir_lowering=False)

        def din(name, shape):
            return nc.dram_tensor(name, shape, F32, kind="ExternalInput")

        self.x_in = din("x", [TOK, DM])
        self.waq = din("waqT", [DM, DM])
        self.wak = din("wakT", [DM, DM])
        self.wav = din("wavT", [DM, DM])
        self.wao = din("waoT", [DM, DM])
        self.wbq = din("wbq", [DM, DM])
        self.wbk = din("wbk", [DM, DM])
        self.wbv = din("wbv", [DM, DM])
        self.wbo = din("wbo", [DM, DM])
        self.wbrel = din("wbrel", [RFEAT, DM])
        self.wbreb = din("wbreb", [NH_B, HD_B])
        self.wbrpb = din("wbrpb", [NH_B, HD_B])
        self.boutrow = din("boutrow", [1, DM])
        self.posT = din("posT", [RFEAT, NPOS])
        self.w1T = din("w1T", [DM, DH])
        self.w2T = din("w2T", [DM, DH])
        self.h1T = din("h1T", [DH, DH])
        self.h2T = din("h2T", [DH, DH])
        self.w3T = din("w3T", [DH, DM])
        self.s1l = din("s1l", [P, NOC_DH])
        self.s2l = din("s2l", [P, NOC_DH])
        self.hs1l = din("hs1l", [P, NOC_DH])
        self.hs2l = din("hs2l", [P, NOC_DH])
        self.s3row = din("s3row", [1, DM])

        self.y_out = nc.dram_tensor("y", [TOK, DM], F32, kind="ExternalOutput")
        self.xs = nc.dram_tensor("xs", [TOK, DM], F32, kind="Internal")
        self.c_w1 = nc.dram_tensor("c_w1", [NOC_DH * P * DM], BF16, kind="Internal")
        self.c_w2 = nc.dram_tensor("c_w2", [NOC_DH * P * DM], BF16, kind="Internal")
        self.c_h1 = nc.dram_tensor("c_h1", [NOC_DH * P * DH], BF16, kind="Internal")
        self.c_h2 = nc.dram_tensor("c_h2", [NOC_DH * P * DH], BF16, kind="Internal")
        self.c_w3 = nc.dram_tensor("c_w3", [DH * DM], BF16, kind="Internal")
        self.c_relkT = nc.dram_tensor("c_relkT", [NH_B * P * NPOS], BF16,
                                      kind="Internal")
        self.NMRING = 6
        self.m_ring = nc.dram_tensor("m_ring", [self.NMRING * P * RWIN], BF16,
                                     kind="Internal")
        self.b = Builder(nc)

    def build(self):
        b, nc = self.b, self.nc
        with ExitStack() as stack:
            def sbuf(name, shape, dtype):
                return stack.enter_context(nc.sbuf_tensor(name, shape, dtype))

            self.identb = sbuf("identb", [P, P], BF16)
            self.s1s = sbuf("s1s", [P, NOC_DH], F32)
            self.s2s = sbuf("s2s", [P, NOC_DH], F32)
            self.hs1s = sbuf("hs1s", [P, NOC_DH], F32)
            self.hs2s = sbuf("hs2s", [P, NOC_DH], F32)
            self.boutbc = sbuf("boutbc", [P, DM], F32)
            self.constv = sbuf("constv", [P, 4], F32)
            self.rebs = sbuf("rebs", [P, NH_B], F32)
            self.rpbs = sbuf("rpbs", [P, NH_B], F32)
            self.dbgbuf = sbuf("dbgbuf", [P, 40], F32)

            self.emit_init(stack)
            b.wait("tensor", self.init_pe_gate)
            b.wait("scalar", self.init_act_gate)

            if not self._tap_hit:
                if "attn_a" in self.phases:
                    self.emit_attn(stack, variant="a", src=self.x_in,
                                   dst=self.xs)
                else:
                    self.emit_copy(self.x_in, self.xs)
                b.wait("gpsimd", self.init_totals)
                b.barrier()

            if not self._tap_hit and "ffn1" in self.phases:
                self.emit_ffn(stack, src=self.xs, dst=self.xs, tag="f1")
                b.barrier()

            if not self._tap_hit and "attn_b" in self.phases:
                self.emit_attn(stack, variant="b", src=self.xs, dst=self.xs)
                b.barrier()

            if not self._tap_hit:
                if "ffn2" in self.phases:
                    self.emit_ffn(stack, src=self.xs, dst=self.y_out, tag="f2")
                else:
                    self.emit_copy(self.xs, self.y_out)
            b.barrier()
            if getattr(self, "dbg_snap", None) is not None:
                b.dma("gpsimd", self.y_out.ap()[0:P, 0:40],
                      self.dbgbuf[:], "dbgd",
                      wait=[self._dbg1, self._dbg2])
                b.wait("gpsimd", [("dbgd", b.counts["dbgd"])])
            b.emit(stack)
        return nc

    def _tapd(self, buf, width, wait, row0=0):
        b = self.b
        self._tap_hit = True
        last = None
        nrt = max(1, width // DM)
        for t in range(nrt):
            w = min(DM, width)
            last = b.dma("gpsimd",
                         self.y_out.ap()[(row0 + t) * P:(row0 + t + 1) * P, 0:w],
                         buf[:, t * DM:t * DM + w], "tapd", wait=wait)
        b.wait("gpsimd", [last])

    def emit_copy(self, src, dst):
        b = self.b
        last = None
        for t in range(NT):
            last = b.dma("gpsimd",
                         dst.ap()[t * P:(t + 1) * P, :],
                         src.ap()[t * P:(t + 1) * P, :], "cp")
        b.wait("gpsimd", [last])

    def emit_init(self, stack):
        b, nc = self.b, self.nc

        b.op("vector", lambda e: e.memset(self.constv[:, 0:1], EPS))
        b.op("vector", lambda e: e.memset(self.constv[:, 1:2], 0.125))
        cv = b.op("vector", lambda e: e.memset(self.constv[:, 2:3],
                                               float(HD_B) ** -0.5))
        b.op("gpsimd", lambda e: e.memset(self.identb[:], 1.0))
        idt = b.op("gpsimd", lambda e: e.affine_select(
            self.identb[:], self.identb[:], [[1, P]], ALU.is_equal, 0.0,
            base=0, channel_multiplier=-1))

        for src, buf in ((self.s1l, self.s1s), (self.s2l, self.s2s),
                         (self.hs1l, self.hs1s), (self.hs2l, self.hs2s)):
            b.dma("gpsimd", buf[:], src.ap(), "ig")
        for h in range(NH_B):
            b.dma("gpsimd", self.rebs[:, h:h + 1],
                  self.wbreb.ap()[h:h + 1, :], "ig")
            b.dma("gpsimd", self.rpbs[:, h:h + 1],
                  self.wbrpb.ap()[h:h + 1, :], "ig")

        def sbufh(name, shape, dtype):
            return stack.enter_context(nc.sbuf_tensor(name, shape, dtype))

        ones_t = sbufh("i_ones", [1, P], BF16)
        brow = sbufh("i_brow", [1, DM], BF16)
        s3rb = sbufh("i_s3rb", [1, DM], BF16)
        s3bc = sbufh("i_s3b", [P, DM], F32)
        stf = [sbufh("i_stgf0", [P, DM], F32)]
        w3o = [sbufh("i_w3b0", [P, DM], BF16)]

        with nc.psum_tensor("i_ps", [P, DM], F32) as ips:
            on = b.op("vector", lambda e: e.memset(ones_t[:], 1.0))
            tb = b.dma("gpsimd", brow[:], self.boutrow.ap(), "ibrow")
            d0 = b.dma("gpsimd", s3rb[:], self.s3row.ap(), "is3r")
            b.op("tensor", lambda e: e.matmul(ips[:, 0:512], ones_t[:],
                                              brow[:, 0:512], start=True,
                                              stop=True),
                 wait=[tb, d0, on])
            mk = b.op("tensor", lambda e: e.matmul(ips[:, 512:1024], ones_t[:],
                                                   brow[:, 512:1024],
                                                   start=True, stop=True),
                      inc="ipe")
            d1 = b.op("vector", lambda e: e.tensor_copy(self.boutbc[:], ips[:]),
                      wait=[mk])
            b.op("tensor", lambda e: e.matmul(ips[:, 0:512], ones_t[:],
                                              s3rb[:, 0:512], start=True,
                                              stop=True), wait=[d1])
            mk = b.op("tensor", lambda e: e.matmul(ips[:, 512:1024], ones_t[:],
                                                   s3rb[:, 512:1024],
                                                   start=True, stop=True),
                      inc="ipe")
            s3_ready = b.op("vector", lambda e: e.tensor_copy(s3bc[:], ips[:]),
                            wait=[mk])

        self.init_pe_gate = [d1, s3_ready, idt]
        self.init_act_gate = [cv, d1]

        stf_sem = ["istf0"]
        cw3_sem = ["icw30"]
        stf_free = [None]
        w3o_free = [None]
        for ic in range(DH // P):
            sl = 0
            w = [stf_free[sl]] if stf_free[sl] else []
            t = b.dma("gpsimd", stf[sl][:],
                      self.w3T.ap()[ic * P:(ic + 1) * P, :],
                      stf_sem[sl], wait=w)
            dw = [t, s3_ready]
            if w3o_free[sl]:
                dw.append(w3o_free[sl])
            d = b.op("vector",
                     lambda e, sl=sl: e.tensor_mul(w3o[sl][:], stf[sl][:],
                                                   s3bc[:]),
                     wait=dw)
            stf_free[sl] = d
            cwt = b.dma("gpsimd",
                        bass.AP(self.c_w3, ic * P * DM, [[DM, P], [1, DM]]),
                        w3o[sl][:], cw3_sem[sl], wait=[d])
            w3o_free[sl] = cwt

        NDSEM = 8
        cd_sem = ["icd%d" % k for k in range(NDSEM)]
        u = 0
        for src_t, dst_t, I, O in ((self.w1T, self.c_w1, DM, DH),
                                   (self.w2T, self.c_w2, DM, DH),
                                   (self.h1T, self.c_h1, DH, DH),
                                   (self.h2T, self.c_h2, DH, DH)):
            noc = O // P
            for ic in range(I // P):
                b.dma("gpsimd",
                      bass.AP(dst_t, ic * P,
                              [[I, P], [P * I, noc], [1, P]]),
                      src_t.ap()[ic * P:(ic + 1) * P, :].rearrange(
                          "p (oc o) -> p oc o", oc=noc),
                      cd_sem[u % NDSEM])
                u += 1

        self.init_totals = [(s, b.counts[s]) for s in
                            (["ig", "ibrow", "is3r", "istf0",
                              "icw30"] + cd_sem)
                            if s in b.counts]

    def emit_ffn(self, stack, src, dst, tag):
        b, nc = self.b, self.nc
        identb = self.identb
        s1s, s2s, hs1s, hs2s = self.s1s, self.s2s, self.hs1s, self.hs2s

        pe = "fpe"
        f_w1 = "ffw1"
        f_w2 = "ffw2"
        f_h = "ffh"
        f_w3 = "ffw3"

        with ExitStack() as st:
            def sbuf(name, shape, dtype):
                return st.enter_context(nc.sbuf_tensor(tag + name, shape, dtype))

            xg = [sbuf("xg0", [P, 4 * DM], F32), sbuf("xg1", [P, 4 * DM], F32)]
            outg = sbuf("outg", [P, 4 * DM], F32)
            usq = sbuf("usq", [P, DM], BF16)
            ub = [sbuf("ub0", [P, DM], BF16), sbuf("ub1", [P, DM], BF16)]
            u_fm = sbuf("ufm", [P, NIC_DM * GT], BF16)
            y1 = sbuf("y1", [P, NOC_DH * GT], BF16)
            y2 = sbuf("y2", [P, NOC_DH * GT], BF16)
            t1b = [sbuf("t1b0", [P, GT], BF16), sbuf("t1b1", [P, GT], BF16)]
            t2b = [sbuf("t2b0", [P, GT], BF16), sbuf("t2b1", [P, GT], BF16)]
            stats = sbuf("stats", [P, 32], F32)

            x_sem = ["fxl0", "fxl1"]
            r_w1 = Ring(b, "frw1",
                        [sbuf("w1b0", [P, DM], BF16), sbuf("w1b1", [P, DM], BF16)])
            r_w2 = Ring(b, "frw2",
                        [sbuf("w2b0", [P, DM], BF16), sbuf("w2b1", [P, DM], BF16)])
            r_h = Ring(b, "frh",
                       [sbuf("hb0", [P, DH], BF16), sbuf("hb1", [P, DH], BF16),
                        sbuf("hb2", [P, DH], BF16)])
            r_w3 = Ring(b, "frw3",
                        [sbuf("w3b0", [P, DM], BF16), sbuf("w3b1", [P, DM], BF16)])

            pW = [st.enter_context(nc.psum_tensor(tag + "pW0", [P, DM], F32)),
                  st.enter_context(nc.psum_tensor(tag + "pW1", [P, DM], F32))]
            pT = [st.enter_context(nc.psum_tensor(tag + "pT0", [P, DM], F32)),
                  st.enter_context(nc.psum_tensor(tag + "pT1", [P, DM], F32))]

            res_by_grp = {}
            out_tok = {}
            last_w2_read_ufm = None
            last_w3_read_y3 = None
            t1b_free = [None, None]
            ufm_evac = {}
            tp_done = {}
            res_last = None

            for grp in range(NGRP):
                slot = grp % 2
                rows = src.ap()[grp * GT:(grp + 1) * GT, :]
                rows_dst = dst.ap()[grp * GT:(grp + 1) * GT, :]
                w = []
                if grp >= 2:
                    w.append(res_by_grp[grp - 2])
                xld = b.dma("gpsimd",
                            xg[slot][:].rearrange("p (t d) -> p t d", t=4),
                            rows.rearrange("(t p) d -> p t d", p=P),
                            x_sem[slot], wait=w)

                for t in range(4):
                    xt = xg[slot][:, t * DM:(t + 1) * DM]
                    c0 = 8 * t
                    s_sum = stats[:, c0:c0 + 1]
                    s_sq = stats[:, c0 + 1:c0 + 2]
                    s_m = stats[:, c0 + 2:c0 + 3]
                    s_v = stats[:, c0 + 3:c0 + 4]
                    s_r0 = stats[:, c0 + 4:c0 + 5]
                    s_r = stats[:, c0 + 5:c0 + 6]
                    s_m2 = stats[:, c0 + 6:c0 + 7]
                    s_nb = stats[:, c0 + 7:c0 + 8]
                    aw = [xld]
                    a1 = b.op("scalar",
                              lambda e, xt=xt, s_sq=s_sq: e.activation(
                                  usq[:], xt, AF.Square, accum_out=s_sq),
                              wait=aw)
                    b.op("vector",
                         lambda e, xt=xt, s_sum=s_sum: e.reduce_sum(
                             s_sum, xt, AX.X), wait=aw)
                    b.op("vector", lambda e, s_m=s_m, s_sum=s_sum:
                         e.tensor_scalar_mul(s_m, s_sum, 1.0 / DM))
                    b.op("vector", lambda e, s_v=s_v, s_sq=s_sq:
                         e.tensor_scalar_mul(s_v, s_sq, 1.0 / DM),
                         wait=[a1])
                    b.op("vector", lambda e, s_m2=s_m2, s_m=s_m:
                         e.tensor_mul(s_m2, s_m, s_m))
                    d2 = b.op("vector", lambda e, s_v=s_v, s_m2=s_m2:
                              e.tensor_sub(s_v, s_v, s_m2))
                    a2 = b.op("scalar", lambda e, s_v=s_v, s_r0=s_r0:
                              e.activation(s_r0, s_v, AF.Sqrt,
                                           bias=self.constv[:, 0:1]),
                              wait=[d2])
                    b.op("vector", lambda e, s_r=s_r, s_r0=s_r0:
                         e.reciprocal(s_r, s_r0), wait=[a2])
                    d4 = b.op("vector", lambda e, s_nb=s_nb, s_m=s_m, s_r=s_r:
                              e.tensor_scalar(s_nb, s_m, s_r, -1.0, ALU.mult,
                                              ALU.mult))
                    uw = [d4]
                    key = (grp, t - 2) if t >= 2 else (grp - 1, t + 2)
                    if key in tp_done:
                        uw.append(tp_done[key])
                    du = b.op("vector",
                              lambda e, xt=xt, t=t, s_r=s_r, s_nb=s_nb:
                              e.tensor_scalar(ub[t % 2][:], xt, s_r, s_nb,
                                              ALU.mult, ALU.add),
                              wait=uw)
                    tw = [du]
                    if key in ufm_evac:
                        tw.append(ufm_evac[key])
                    tp = None
                    for ic in range(NIC_DM):
                        tp = b.op("tensor",
                                  lambda e, t=t, ic=ic: e.matmul(
                                      pT[t % 2][:, ic * P:(ic + 1) * P],
                                      ub[t % 2][:, ic * P:(ic + 1) * P],
                                      identb[:], start=True, stop=True),
                                  wait=tw if ic == 0 else [],
                                  inc=pe if ic == NIC_DM - 1 else None)
                    tp_done[(grp, t)] = tp
                    ew = [tp]
                    if grp > 0 and t == 0 and last_w2_read_ufm:
                        ew.append(last_w2_read_ufm)
                    ufm_out = u_fm[:].rearrange(
                        "p (ic g) -> p ic g", ic=NIC_DM)[:, :, t * P:(t + 1) * P]
                    pt_src = pT[t % 2][:].rearrange(
                        "p (ic q) -> p ic q", ic=NIC_DM)
                    ufm_evac[(grp, t)] = b.op(
                        "vector",
                        lambda e, ufm_out=ufm_out, pt_src=pt_src:
                        e.tensor_copy(ufm_out, pt_src),
                        wait=ew)
                ufm_ready = ufm_evac[(grp, 3)]
                if self.tap == ("ufm", grp):
                    self._tapd(u_fm, NIC_DM * GT, [ufm_ready])
                    return

                y1_mult = {}
                silu_evac = {}
                for oc in range(NOC_DH):
                    sl1, w1g = r_w1.load(
                        bass.AP(self.c_w1, oc * P * DM, [[DM, P], [1, DM]]))
                    sl2, w2g = r_w2.load(
                        bass.AP(self.c_w2, oc * P * DM, [[DM, P], [1, DM]]))
                    psA = pW[oc % 2][:, 0:GT]
                    psB = pW[oc % 2][:, GT:2 * GT]
                    mw = [w1g, ufm_ready]
                    if oc >= 2:
                        mw.append(silu_evac[oc - 2][0])
                    elif res_last:
                        mw.append(res_last)
                    w1_tok = None
                    for ic in range(NIC_DM):
                        w1_tok = b.op(
                            "tensor",
                            lambda e, ic=ic, sl1=sl1, psA=psA: e.matmul(
                                psA, r_w1.bufs[sl1][:, ic * P:(ic + 1) * P],
                                u_fm[:, ic * GT:(ic + 1) * GT],
                                start=(ic == 0), stop=(ic == NIC_DM - 1)),
                            wait=mw if ic == 0 else [],
                            inc=f_w1 if ic == NIC_DM - 1 else None)
                    r_w1.set_free(sl1, w1_tok)
                    mw = [w2g]
                    if oc >= 2:
                        mw.append(silu_evac[oc - 2][1])
                    w2_tok = None
                    for ic in range(NIC_DM):
                        w2_tok = b.op(
                            "tensor",
                            lambda e, ic=ic, sl2=sl2, psB=psB: e.matmul(
                                psB, r_w2.bufs[sl2][:, ic * P:(ic + 1) * P],
                                u_fm[:, ic * GT:(ic + 1) * GT],
                                start=(ic == 0), stop=(ic == NIC_DM - 1)),
                            wait=mw if ic == 0 else [],
                            inc=f_w2 if ic == NIC_DM - 1 else None)
                    r_w2.set_free(sl2, w2_tok)
                    last_w2_read_ufm = w2_tok
                    ew = [w1_tok]
                    if oc >= 2:
                        ew.append(y1_mult[oc - 2])
                    elif t1b_free[oc % 2]:
                        ew.append(t1b_free[oc % 2])
                    e1 = b.op("scalar",
                              lambda e, oc=oc, psA=psA: e.activation(
                                  t1b[oc % 2][:], psA, AF.Silu,
                                  scale=s1s[:, oc:oc + 1]),
                              wait=ew)
                    e2 = b.op("scalar",
                              lambda e, oc=oc, psB=psB: e.activation(
                                  t2b[oc % 2][:], psB, AF.Copy,
                                  scale=s2s[:, oc:oc + 1]),
                              wait=[w2_tok])
                    silu_evac[oc] = (e1, e2)
                    mw = [e2]
                    if oc == 0 and last_w3_read_y3:
                        mw.append(last_w3_read_y3)
                    y1_mult[oc] = b.op(
                        "vector",
                        lambda e, oc=oc: e.tensor_mul(
                            y1[:, oc * GT:(oc + 1) * GT], t1b[oc % 2][:],
                            t2b[oc % 2][:]),
                        wait=mw)
                t1b_free = [y1_mult[NOC_DH - 2], y1_mult[NOC_DH - 1]]
                y1_done = y1_mult[NOC_DH - 1]
                if self.tap == ("y1", grp):
                    self._tapd(y1, NOC_DH * GT, [y1_done])
                    return

                prev_done = y1_done
                h_in = [y1, y2]
                caches = [self.c_h1, self.c_h2]
                scales = [hs1s, hs2s]
                outs = [y2, y1]
                for stage in range(2):
                    gelu_evac = {}
                    for oc in range(NOC_DH):
                        slh, hg = r_h.load(
                            bass.AP(caches[stage], oc * P * DH,
                                    [[DH, P], [1, DH]]))
                        ps = pW[oc % 2][:, 0:GT]
                        mw = [hg, prev_done]
                        if oc >= 2:
                            mw.append(gelu_evac[oc - 2])
                        elif stage == 0:
                            mw.append(silu_evac[NOC_DH - 2 + oc][0])
                        h_tok = None
                        for ic in range(NOC_DH):
                            h_tok = b.op(
                                "tensor",
                                lambda e, ic=ic, slh=slh, ps=ps, stage=stage:
                                e.matmul(
                                    ps, r_h.bufs[slh][:, ic * P:(ic + 1) * P],
                                    h_in[stage][:, ic * GT:(ic + 1) * GT],
                                    start=(ic == 0), stop=(ic == NOC_DH - 1)),
                                wait=mw if ic == 0 else [],
                                inc=f_h if ic == NOC_DH - 1 else None)
                        r_h.set_free(slh, h_tok)
                        gelu_evac[oc] = b.op(
                            "scalar",
                            lambda e, oc=oc, ps=ps, stage=stage: e.activation(
                                outs[stage][:, oc * GT:(oc + 1) * GT], ps,
                                AF.Gelu, scale=scales[stage][:, oc:oc + 1]),
                            wait=[h_tok])
                    prev_done = gelu_evac[NOC_DH - 1]
                    if self.tap == ("y2" if stage == 0 else "y3", grp):
                        self._tapd(outs[stage], NOC_DH * GT, [prev_done])
                        return
                y3_done = prev_done

                res_adds = []
                for hp in range(2):
                    w3_tok = None
                    for ic in range(NOC_DH):
                        sl3, w3g = r_w3.load(
                            bass.AP(self.c_w3, ic * P * DM, [[DM, P], [1, DM]]))
                        mw = [w3g]
                        if ic == 0:
                            mw.append(y3_done)
                            if hp == 1:
                                mw.append(res_adds[-1])
                        for t_loc in range(2):
                            t = hp * 2 + t_loc
                            for half in range(2):
                                ps = pW[t_loc][:, half * GT:(half + 1) * GT]
                                last_mm = (t_loc == 1 and half == 1)
                                w3_tok2 = b.op(
                                    "tensor",
                                    lambda e, ic=ic, sl3=sl3, t=t, ps=ps,
                                    half=half: e.matmul(
                                        ps,
                                        y1[:, ic * GT + t * P:
                                           ic * GT + (t + 1) * P],
                                        r_w3.bufs[sl3][:,
                                                       half * GT:
                                                       (half + 1) * GT],
                                        start=(ic == 0),
                                        stop=(ic == NOC_DH - 1)),
                                    wait=mw if (t_loc == 0 and half == 0)
                                    else [],
                                    inc=f_w3 if last_mm else None)
                                if last_mm:
                                    w3_tok = w3_tok2
                        r_w3.set_free(sl3, w3_tok)
                    w3_done = w3_tok
                    last_w3_read_y3 = w3_done
                    for t_loc in range(2):
                        t = hp * 2 + t_loc
                        rw = [w3_done]
                        if grp > 0 and hp == 0 and t_loc == 0:
                            rw.append(out_tok[grp - 1])
                        r = b.op("vector",
                                 lambda e, t=t, t_loc=t_loc, slot=slot:
                                 e.tensor_add(
                                     outg[:, t * DM:(t + 1) * DM],
                                     pW[t_loc][:],
                                     xg[slot][:, t * DM:(t + 1) * DM]),
                                 wait=rw)
                        res_adds.append(r)
                res_last = res_adds[-1]
                res_by_grp[grp] = res_last
                out_tok[grp] = b.dma("gpsimd",
                                     rows_dst.rearrange("(t p) d -> p t d", p=P),
                                     outg[:].rearrange("p (t d) -> p t d", t=4),
                                     "fxo", wait=[res_last])

            b.wait("gpsimd", [out_tok[NGRP - 1]])

    def emit_attn(self, stack, variant, src, dst):
        b, nc = self.b, self.nc
        is_a = variant == "a"
        tg = "a" if is_a else "t"

        with ExitStack() as ph:
            def sbuf(name, shape, dtype):
                return ph.enter_context(nc.sbuf_tensor(tg + name, shape, dtype))

            qT = sbuf("qT", [P, 8 * SEQ], BF16)
            kT = sbuf("kT", [P, 8 * SEQ], BF16)
            vsb = sbuf("v", [P, NQT * DM], BF16)

            relk_ready = None
            if not is_a:
                with nc.sbuf_tensor(tg + "posT", [P, 2 * NPOS], BF16) as posb, \
                     nc.sbuf_tensor(tg + "wrel", [P, 2 * DM], BF16) as wrelb, \
                     nc.sbuf_tensor(tg + "rko", [P, NPOS], BF16) as rko, \
                     nc.psum_tensor(tg + "rkp", [P, 512], F32) as rkp:
                    t1 = b.dma("gpsimd", posb[:, 0:NPOS],
                               self.posT.ap()[0:P, :], "blpos")
                    t2 = b.dma("gpsimd", posb[0:64, NPOS:2 * NPOS],
                               self.posT.ap()[P:RFEAT, :], "blpos2")
                    t3 = b.dma("gpsimd", wrelb[:, 0:DM],
                               self.wbrel.ap()[0:P, :], "blrel")
                    t3b = b.dma("gpsimd", wrelb[0:64, DM:2 * DM],
                                self.wbrel.ap()[P:RFEAT, :], "blrel2")
                    rk_dma = None
                    rk_prev = None
                    for h in range(NH_B):
                        ev_prev = None
                        for c in range(8):
                            n = 512 if c < 7 else 511
                            mm = None
                            for kc in range(2):
                                rows = P if kc == 0 else 64
                                mm = b.op(
                                    "tensor",
                                    lambda e, h=h, c=c, kc=kc, rows=rows, n=n:
                                    e.matmul(
                                        rkp[:, 0:n],
                                        wrelb[0:rows,
                                              kc * DM + h * P:
                                              kc * DM + (h + 1) * P],
                                        posb[0:rows,
                                             kc * NPOS + c * 512:
                                             kc * NPOS + c * 512 + n],
                                        start=(kc == 0), stop=(kc == 1)),
                                    wait=([t for t in
                                           (t1, t2, t3, t3b, ev_prev, rk_prev)
                                           if t] if kc == 0 else []),
                                    inc="brpe" if kc == 1 else None)
                            ev_prev = b.op(
                                "vector",
                                lambda e, c=c, n=n: e.tensor_copy(
                                    rko[:, c * 512:c * 512 + n], rkp[:, 0:n]),
                                wait=[mm, rk_prev] if rk_prev else [mm])
                        rk_dma = b.dma(
                            "gpsimd",
                            bass.AP(self.c_relkT, h * P * NPOS,
                                    [[NPOS, P], [1, NPOS]]),
                            rko[:], "blrk", wait=[ev_prev])
                        rk_prev = rk_dma
                    relk_ready = rk_dma
                    b.wait("gpsimd", [("blrk", b.counts["blrk"])])

            for s in range(NSEQ):
                self._attn_seq(b, nc, variant, src, dst, s,
                               qT, kT, vsb, relk_ready)
                if self._tap_hit:
                    return

    def _attn_seq(self, b, nc, variant, src, dst, s, qT, kT, vsb, relk_ready):
        identb = self.identb
        is_a = variant == "a"
        tg = ("a" if is_a else "t") + str(s)
        NH = NH_A if is_a else NH_B
        HD = HD_A if is_a else HD_B
        NCH = 8
        row0 = s * SEQ
        niter = NH * NQT
        wq_in = self.waq if is_a else self.wbq
        wk_in = self.wak if is_a else self.wbk
        wv_in = self.wav if is_a else self.wbv
        wo_in = self.wao if is_a else self.wbo

        pe = "ape"
        f_q = "afq"
        f_o = "afo"

        with ExitStack() as p1:
            def sbuf1(name, shape, dtype):
                return p1.enter_context(nc.sbuf_tensor(tg + name, shape, dtype))

            wqs = sbuf1("wq", [P, NCH * DM], BF16)
            wks = sbuf1("wk", [P, NCH * DM], BF16)
            wvs = sbuf1("wv", [P, NCH * DM], BF16)
            xb = [sbuf1("xb0", [P, DM], F32), sbuf1("xb1", [P, DM], F32)]
            usq = sbuf1("usq", [P, DM], BF16)
            ub = [sbuf1("ub0", [P, DM], BF16), sbuf1("ub1", [P, DM], BF16)]
            u_fm = sbuf1("ufm", [P, NCH * GT], BF16)
            stats = sbuf1("stats", [P, 8], F32)
            psQK = [p1.enter_context(nc.psum_tensor(tg + "psQ0", [P, DM], F32)),
                    p1.enter_context(nc.psum_tensor(tg + "psQ1", [P, DM], F32))]
            psV = [p1.enter_context(nc.psum_tensor(tg + "psV0", [P, DM], F32)),
                   p1.enter_context(nc.psum_tensor(tg + "psV1", [P, DM], F32))]

            wtoks = []
            for dsrc, dstb, semn in ((wq_in, wqs, "alwq"),
                                     (wk_in, wks, "alwk"),
                                     (wv_in, wvs, "alwv")):
                ap = bass.AP(dsrc, 0, [[DM, P], [P * DM, NCH], [1, DM]])
                wtoks.append(b.dma("gpsimd", dstb[:].rearrange(
                    "p (ic o) -> p ic o", ic=NCH), ap, semn))
            wq_tok, wk_tok, wv_tok = wtoks

            x_sem = ["axl0", "axl1"]
            ln_last = [None, None]
            tp_done = {}
            ufm_evac = {}
            qk_evac = {}
            v_evac = {}
            last_qkv_read_ufm = None
            for g in range(4):
                for t in range(4):
                    ti = g * 4 + t
                    slot = ti % 2
                    xt = xb[slot][:]
                    w = [ln_last[slot]] if ln_last[slot] else []
                    xld = b.dma("gpsimd", xt,
                                src.ap()[row0 + ti * P:row0 + (ti + 1) * P, :],
                                x_sem[slot], wait=w)
                    s_sum = stats[:, 0:1]
                    s_sq = stats[:, 1:2]
                    s_m = stats[:, 2:3]
                    s_v = stats[:, 3:4]
                    s_r0 = stats[:, 4:5]
                    s_r = stats[:, 5:6]
                    s_m2 = stats[:, 6:7]
                    s_nb = stats[:, 7:8]
                    aw = [xld]
                    a1 = b.op("scalar",
                              lambda e, xt=xt, s_sq=s_sq: e.activation(
                                  usq[:], xt, AF.Square, accum_out=s_sq),
                              wait=aw)
                    b.op("vector",
                         lambda e, xt=xt, s_sum=s_sum: e.reduce_sum(
                             s_sum, xt, AX.X), wait=aw)
                    b.op("vector", lambda e, s_m=s_m, s_sum=s_sum:
                         e.tensor_scalar_mul(s_m, s_sum, 1.0 / DM))
                    b.op("vector", lambda e, s_v=s_v, s_sq=s_sq:
                         e.tensor_scalar_mul(s_v, s_sq, 1.0 / DM),
                         wait=[a1])
                    b.op("vector", lambda e, s_m2=s_m2, s_m=s_m:
                         e.tensor_mul(s_m2, s_m, s_m))
                    d2 = b.op("vector", lambda e, s_v=s_v, s_m2=s_m2:
                              e.tensor_sub(s_v, s_v, s_m2))
                    a2 = b.op("scalar", lambda e, s_v=s_v, s_r0=s_r0:
                              e.activation(s_r0, s_v, AF.Sqrt,
                                           bias=self.constv[:, 0:1]),
                              wait=[d2])
                    b.op("vector", lambda e, s_r=s_r, s_r0=s_r0:
                         e.reciprocal(s_r, s_r0), wait=[a2])
                    d4 = b.op("vector", lambda e, s_nb=s_nb, s_m=s_m, s_r=s_r:
                              e.tensor_scalar(s_nb, s_m, s_r, -1.0, ALU.mult,
                                              ALU.mult))
                    uw = [d4]
                    if (ti - 2) >= 0 and (ti - 2) in tp_done:
                        uw.append(tp_done[ti - 2])
                    du = b.op("vector",
                              lambda e, xt=xt, slot=slot, s_r=s_r, s_nb=s_nb:
                              e.tensor_scalar(ub[slot][:], xt, s_r, s_nb,
                                              ALU.mult, ALU.add),
                              wait=uw)
                    ln_last[slot] = du
                    tw = [du]
                    if (ti - 2) in ufm_evac:
                        tw.append(ufm_evac[ti - 2])
                    if t == 0 and g > 0:
                        tw.append(qk_evac[(1, NCH - 2, g - 1)])
                        tw.append(qk_evac[(1, NCH - 1, g - 1)])
                    tp = None
                    for ic in range(NCH):
                        tp = b.op("tensor",
                                  lambda e, t=t, ic=ic, slot=slot: e.matmul(
                                      psQK[t % 2][:, ic * P:(ic + 1) * P],
                                      ub[slot][:, ic * P:(ic + 1) * P],
                                      identb[:], start=True, stop=True),
                                  wait=tw if ic == 0 else [],
                                  inc=pe if ic == NCH - 1 else None)
                    tp_done[ti] = tp
                    ufm_out = u_fm[:].rearrange(
                        "p (ic gg) -> p ic gg", ic=NCH)[:, :, t * P:(t + 1) * P]
                    pt_src = psQK[t % 2][:].rearrange(
                        "p (a q) -> p a q", a=NCH)
                    ufm_evac[ti] = b.op(
                        "vector",
                        lambda e, ufm_out=ufm_out, pt_src=pt_src:
                        e.tensor_copy(ufm_out, pt_src),
                        wait=[tp])
                ufm_ready = ufm_evac[g * 4 + 3]

                for which, wbuf, wtok, dstT, scale_q in (
                        (0, wqs, wq_tok, qT, True),
                        (1, wks, wk_tok, kT, False)):
                    for oc in range(NCH):
                        psp = psQK[oc % 2][:, which * GT:(which + 1) * GT]
                        mw = [ufm_ready, wtok]
                        pk = ((which, oc - 2, g) if oc >= 2
                              else (1, NCH - 2 + oc, g - 1))
                        if pk in qk_evac:
                            mw.append(qk_evac[pk])
                        mtok = None
                        for ic in range(NCH):
                            mtok = b.op(
                                "tensor",
                                lambda e, oc=oc, ic=ic, wbuf=wbuf, psp=psp:
                                e.matmul(
                                    psp,
                                    wbuf[:, ic * DM + oc * P:
                                         ic * DM + (oc + 1) * P],
                                    u_fm[:, ic * GT:(ic + 1) * GT],
                                    start=(ic == 0), stop=(ic == NCH - 1)),
                                wait=mw if ic == 0 else [],
                                inc=f_q if ic == NCH - 1 else None)
                        qcol = 1 if is_a else 2
                        sc = self.constv[:, qcol:qcol + 1] if scale_q else 1.0
                        qk_evac[(which, oc, g)] = b.op(
                            "scalar",
                            lambda e, oc=oc, g=g, psp=psp, sc=sc, dstT=dstT:
                            e.activation(
                                dstT[:, oc * SEQ + g * GT:
                                     oc * SEQ + (g + 1) * GT],
                                psp, AF.Copy, scale=sc),
                            wait=[mtok])
                for t in range(4):
                    ti = g * 4 + t
                    psp = psV[t % 2]
                    mw = [ufm_ready, wv_tok]
                    pk = ti - 2
                    if pk in v_evac:
                        mw.append(v_evac[pk])
                    mtok = None
                    for ic in range(NCH):
                        for half in range(2):
                            mtok = b.op(
                                "tensor",
                                lambda e, t=t, ic=ic, half=half, psp=psp:
                                e.matmul(
                                    psp[:, half * GT:(half + 1) * GT],
                                    u_fm[:, ic * GT + t * P:
                                         ic * GT + (t + 1) * P],
                                    wvs[:, ic * DM + half * GT:
                                        ic * DM + (half + 1) * GT],
                                    start=(ic == 0), stop=(ic == NCH - 1)),
                                wait=mw if (ic == 0 and half == 0) else [],
                                inc=f_q if (ic == NCH - 1 and half == 1)
                                else None)
                    last_qkv_read_ufm = mtok
                    v_evac[ti] = b.op(
                        "vector",
                        lambda e, ti=ti, psp=psp: e.tensor_copy(
                            vsb[:, ti * DM:(ti + 1) * DM], psp[:]),
                        wait=[mtok])
            qkv_ready = [qk_evac[(0, NCH - 1, 3)], qk_evac[(1, NCH - 1, 3)],
                         v_evac[15]]
        if self.tap == ("qkv" + variant, s):
            self._tapd(qT, NCH * SEQ, qkv_ready)
            self._tapd(kT, NCH * SEQ, qkv_ready, row0=16)
            return

        with ExitStack() as p23:
            afm = p23.enter_context(
                nc.sbuf_tensor(tg + "afm", [P, NCH * SEQ], BF16))
            attn_done = self._attn_scores(
                b, nc, variant, s, qT, kT, vsb, afm, qkv_ready, relk_ready)
            if self._tap_hit:
                return

            with ExitStack() as p3:
                def sbuf3(name, shape, dtype):
                    return p3.enter_context(
                        nc.sbuf_tensor(tg + name, shape, dtype))

                wos = sbuf3("wo", [P, NCH * DM], BF16)
                xres = [sbuf3("xr0", [P, DM], F32), sbuf3("xr1", [P, DM], F32)]
                outr = [sbuf3("or0", [P, DM], F32), sbuf3("or1", [P, DM], F32)]
                psW = [p3.enter_context(
                           nc.psum_tensor(tg + "psW0", [P, DM], F32)),
                       p3.enter_context(
                           nc.psum_tensor(tg + "psW1", [P, DM], F32))]
                wo_tok = b.dma("gpsimd", wos[:].rearrange(
                    "p (ic o) -> p ic o", ic=NCH),
                    bass.AP(wo_in, 0, [[DM, P], [P * DM, NCH], [1, DM]]),
                    "alwo", wait=[attn_done])
                xr_sem = ["axr0", "axr1"]
                add_tok = {}
                st_tok = {}
                for qt in range(NQT):
                    slot = qt % 2
                    w = [add_tok[qt - 2]] if qt >= 2 else [attn_done]
                    xld = b.dma("gpsimd", xres[slot][:],
                                src.ap()[row0 + qt * P:row0 + (qt + 1) * P, :],
                                xr_sem[slot], wait=w)
                    mw = [attn_done, wo_tok]
                    if qt >= 2:
                        mw.append(add_tok[qt - 2])
                    mtok = None
                    for ic in range(NCH):
                        for half in range(2):
                            mtok = b.op(
                                "tensor",
                                lambda e, qt=qt, ic=ic, half=half, slot=slot:
                                e.matmul(
                                    psW[slot][:, half * GT:(half + 1) * GT],
                                    afm[:, ic * SEQ + qt * P:
                                        ic * SEQ + (qt + 1) * P],
                                    wos[:, ic * DM + half * GT:
                                        ic * DM + (half + 1) * GT],
                                    start=(ic == 0), stop=(ic == NCH - 1)),
                                wait=mw if (ic == 0 and half == 0) else [],
                                inc=f_o if (ic == NCH - 1 and half == 1)
                                else None)
                    aw = [mtok, xld]
                    if qt >= 2 and (qt - 2) in st_tok:
                        aw.append(st_tok[qt - 2])
                    if is_a:
                        add_tok[qt] = b.op(
                            "vector",
                            lambda e, slot=slot: e.tensor_add(
                                outr[slot][:], psW[slot][:], xres[slot][:]),
                            wait=aw)
                    else:
                        t0 = b.op("vector",
                                  lambda e, slot=slot: e.tensor_add(
                                      outr[slot][:], psW[slot][:],
                                      self.boutbc[:]),
                                  wait=aw)
                        add_tok[qt] = b.op(
                            "vector",
                            lambda e, slot=slot: e.tensor_add(
                                outr[slot][:], outr[slot][:], xres[slot][:]),
                            wait=[t0])
                    st_tok[qt] = b.dma(
                        "gpsimd",
                        dst.ap()[row0 + qt * P:row0 + (qt + 1) * P, :],
                        outr[slot][:], "ast" + str(slot),
                        wait=[add_tok[qt]])
                b.wait("gpsimd", [st_tok[NQT - 1], st_tok[NQT - 2]])

    def _attn_scores(self, b, nc, variant, s, qT, kT, vsb, afm, qkv_ready,
                     relk_ready):
        identb = self.identb
        is_a = variant == "a"
        tg = ("a" if is_a else "t") + str(s) + "s"
        NH = NH_A if is_a else NH_B
        niter = NH * NQT
        f_s = "afs"
        f_t = "aft"
        f_v = "afv"
        f_m = "afm_"

        NPB = 3 if is_a else 2
        with ExitStack() as p2:
            def sbuf2(name, shape, dtype):
                return p2.enter_context(nc.sbuf_tensor(tg + name, shape, dtype))

            St_sb = [sbuf2("St0", [P, SEQ], BF16), sbuf2("St1", [P, SEQ], BF16)]
            P_sb = [sbuf2("P%d" % k, [P, SEQ], BF16) for k in range(NPB)]
            PT_sb = [sbuf2("PT%d" % k, [P, SEQ], BF16) for k in range(NPB)]
            sst = [sbuf2("ss%d" % k, [P, 4], F32) for k in range(NPB)]
            ps_S = p2.enter_context(nc.psum_tensor(tg + "psS", [P, SEQ], F32))
            ps_Tb = p2.enter_context(nc.psum_tensor(tg + "psT", [P, SEQ], BF16))
            ps_PV = p2.enter_context(nc.psum_tensor(tg + "psPV", [P, P], F32))
            if not is_a:
                ps_Mt = p2.enter_context(
                    nc.psum_tensor(tg + "psMt", [P, P], F32))
                M_sb = [sbuf2("M0", [P, RWIN], BF16),
                        sbuf2("M1", [P, RWIN], BF16)]
                RS_sb = [sbuf2("R0", [P, SEQ], BF16),
                         sbuf2("R1", [P, SEQ], BF16)]
                qa_sb = [sbuf2("qa0", [P, P], BF16), sbuf2("qa1", [P, P], BF16)]
                qb_sb = [sbuf2("qb0", [P, P], BF16), sbuf2("qb1", [P, P], BF16)]
                relk = [sbuf2("rk0", [P, NPOS], BF16),
                        sbuf2("rk1", [P, NPOS], BF16)]
                relk_sem = ["ark0", "ark1"]
                relk_tok = [None, None]
                mw_sem = ["amw0", "amw1"]
                ac_sem = ["aac0", "aac1"]
                mring_free = {}
                msb_free = [None, None]
                rs_tok = {}
                mev_tok = {}
                qbias_free = {0: [None, None], 1: [None, None]}

            s_tok = {}
            ev_tok = {}
            exp_tok = {}
            norm_tok = {}
            tp_tok = {}
            ptev_tok = {}
            pv_tok = {}
            pvev_tok = {}

            def idx_hq(i):
                return i // NQT, i % NQT

            def emit_qbias(i, which):
                h, qt = idx_hq(i)
                buf = (qa_sb if which == 0 else qb_sb)[i % 2]
                bias = (self.rebs if which == 0 else self.rpbs)[:, h:h + 1]
                gate = list(qkv_ready)
                if qbias_free[which][i % 2]:
                    gate.append(qbias_free[which][i % 2])
                return b.op("vector",
                            lambda e, h=h, qt=qt, buf=buf, bias=bias:
                            e.tensor_scalar_add(
                                buf[:], qT[:, h * SEQ + qt * P:
                                           h * SEQ + (qt + 1) * P], bias),
                            wait=gate)

            def emit_S(i):
                h, qt = idx_hq(i)
                is_a_l = is_a
                if is_a:
                    ch, rb = h // 2, 64 * (h % 2)
                    lhs_fn = (lambda ch=ch, rb=rb, qt=qt:
                              qT[rb:rb + 64, ch * SEQ + qt * P:
                                 ch * SEQ + (qt + 1) * P])
                    rhs_fn = (lambda c, ch=ch, rb=rb: kT[
                        rb:rb + 64,
                        ch * SEQ + c * 512:ch * SEQ + (c + 1) * 512])
                    qa_tok = None
                else:
                    qa_tok = emit_qbias(i, 0)
                    lhs_fn = lambda i=i: qa_sb[i % 2][:]
                    rhs_fn = (lambda c, h=h: kT[
                        :, h * SEQ + c * 512:h * SEQ + (c + 1) * 512])
                toks = []
                for c in range(4):
                    w = []
                    if c == 0:
                        w = list(qkv_ready)
                        if qa_tok:
                            w.append(qa_tok)
                        if not is_a and i + 1 in mev_tok:
                            w += list(mev_tok[i + 1][:2])
                        elif i >= 1:
                            w += list(ev_tok[i - 1])
                    toks.append(b.op(
                        "tensor",
                        lambda e, c=c, lhs_fn=lhs_fn, rhs_fn=rhs_fn:
                        e.matmul(ps_S[:, c * 512:(c + 1) * 512],
                                 lhs_fn(), rhs_fn(c),
                                 start=True, stop=is_a_l),
                        wait=w, inc=(f_s if (c == 3 and is_a_l) else None)))
                if not is_a:
                    qbias_free[0][i % 2] = toks[-1]
                    last = None
                    for c in range(4):
                        w = [rs_tok[i]] if c == 0 else []
                        last = b.op(
                            "tensor",
                            lambda e, c=c, i=i: e.matmul(
                                ps_S[:, c * 512:(c + 1) * 512],
                                identb[:],
                                RS_sb[i % 2][:, c * 512:(c + 1) * 512],
                                start=False, stop=True),
                            wait=w, inc=f_s if c == 3 else None)
                    s_tok[i] = last
                else:
                    s_tok[i] = toks[-1]

            def emit_M(j):
                h, qt = idx_hq(j)
                r_lo = (SEQ - P) - qt * P
                qb_tok = emit_qbias(j, 1)
                slot_rk = h % 2
                mslot = j % 2
                dslot = j % self.NMRING
                mtok = None
                for c in range(4):
                    w = []
                    if c == 0:
                        w = [qb_tok, relk_tok[slot_rk]]
                        if (j - 2) >= 0 and (j - 2) in ev_tok:
                            w += list(ev_tok[j - 2])
                        elif j == 1:
                            w += [mev_tok[0][0], mev_tok[0][1]]
                    mtok = b.op(
                        "tensor",
                        lambda e, c=c, mslot=mslot, r_lo=r_lo,
                        slot_rk=slot_rk: e.matmul(
                            ps_S[:, c * 512:(c + 1) * 512],
                            qb_sb[mslot][:],
                            relk[slot_rk][:, r_lo + c * 512:
                                          r_lo + (c + 1) * 512],
                            start=True, stop=True),
                        wait=w, inc=f_m if c == 3 else None)
                mt_tok = b.op(
                    "tensor",
                    lambda e, mslot=mslot, r_lo=r_lo, slot_rk=slot_rk:
                    e.matmul(
                        ps_Mt[:, 0:127],
                        qb_sb[mslot][:],
                        relk[slot_rk][:, r_lo + 2048:r_lo + 2175],
                        start=True, stop=True),
                    wait=[mev_tok[j - 1][2]] if j >= 1 else [],
                    inc=f_m)
                qbias_free[1][j % 2] = mt_tok
                w1_ = [mtok]
                if msb_free[mslot]:
                    w1_.append(msb_free[mslot])
                me_d = b.op("vector",
                            lambda e, mslot=mslot: e.tensor_copy(
                                M_sb[mslot][:, 0:1024], ps_S[:, 0:1024]),
                            wait=w1_)
                me_a = b.op("scalar",
                            lambda e, mslot=mslot: e.activation(
                                M_sb[mslot][:, 1024:2048], ps_S[:, 1024:2048],
                                AF.Copy),
                            wait=[mtok] + ([msb_free[mslot]]
                                           if msb_free[mslot] else []))
                me_t = b.op("scalar",
                            lambda e, mslot=mslot: e.activation(
                                M_sb[mslot][:, 2048:2175], ps_Mt[:, 0:127],
                                AF.Copy),
                            wait=[mt_tok])
                mev_tok[j] = (me_d, me_a, me_t)
                wgate = [me_d, me_a, me_t]
                if dslot in mring_free:
                    wgate.append(mring_free[dslot])
                wr = b.dma("gpsimd",
                           bass.AP(self.m_ring, dslot * P * RWIN,
                                   [[RWIN, P], [1, RWIN]]),
                           M_sb[mslot][:], mw_sem[mslot], wait=wgate)
                msb_free[mslot] = wr
                return wr

            wr_tok = {}
            mx_tok = {}

            def emit_ev(j):
                par = j % 2
                w = [s_tok[j]]
                if j >= 2:
                    w.append(exp_tok[j - 2])
                ev = b.op("scalar",
                          lambda e, par=par: e.activation(
                              St_sb[par][:], ps_S[:], AF.Copy),
                          wait=w)
                ev_tok[j] = (ev,)

            def emit_mx(j):
                spar = j % 2
                mneg = sst[j % NPB][:, 0:1]
                mx_tok[j] = b.op(
                    "vector",
                    lambda e, spar=spar, mneg=mneg: e.reduce_max(
                        mneg, St_sb[spar][:], AX.X, negate=True),
                    wait=list(ev_tok[j]))

            def emit_rd(j):
                dslot = j % self.NMRING
                rw = [wr_tok[j]]
                if (j - 2) >= 0:
                    rw.append(s_tok[j - 2])
                rd = b.dma("gpsimd", RS_sb[j % 2][:],
                           bass.AP(self.m_ring, dslot * P * RWIN + 127,
                                   [[RWIN - 1, P], [1, SEQ]]),
                           ac_sem[j % 2], wait=rw)
                rs_tok[j] = rd
                mring_free[dslot] = rd

            if not is_a:
                relk_tok[0] = b.dma(
                    "gpsimd", relk[0][:],
                    bass.AP(self.c_relkT, 0, [[NPOS, P], [1, NPOS]]),
                    relk_sem[0], wait=[relk_ready] + list(qkv_ready))
                wr_tok[0] = emit_M(0)
                if niter > 1:
                    wr_tok[1] = emit_M(1)
                emit_rd(0)
            emit_S(0)
            emit_ev(0)
            emit_mx(0)
            if not is_a and niter > 2:
                wr_tok[2] = emit_M(2)
            if not is_a and niter > 1:
                emit_rd(1)
            if niter > 1:
                emit_S(1)
            if not is_a and niter > 2:
                emit_rd(2)

            for i in range(niter + 2):
                par = i % NPB
                spar = i % 2
                if not is_a:
                    j4 = i + 4
                    if j4 < niter and j4 % NQT == 0:
                        hh = j4 // NQT
                        gate = []
                        j_last = (hh - 2) * NQT + NQT - 1
                        if j_last in mev_tok:
                            gate = [mev_tok[j_last][2]]
                        elif hh == 1:
                            gate = list(qkv_ready)
                        relk_tok[hh % 2] = b.dma(
                            "gpsimd", relk[hh % 2][:],
                            bass.AP(self.c_relkT, hh * P * NPOS,
                                    [[NPOS, P], [1, NPOS]]),
                            relk_sem[hh % 2], wait=gate)

                if i + 1 < niter:
                    emit_ev(i + 1)
                    emit_mx(i + 1)
                if not is_a and i + 3 < niter:
                    wr_tok[i + 3] = emit_M(i + 3)
                if i + 2 < niter:
                    emit_S(i + 2)
                if not is_a and i + 3 < niter:
                    emit_rd(i + 3)

                if i < niter:
                    h, qt = idx_hq(i)
                    ch = h // 2 if is_a else h
                    lsum = sst[par][:, 1:2]
                    rl = sst[par][:, 2:3]
                    mneg = sst[i % NPB][:, 0:1]
                    ew = [mx_tok[i]]
                    if i >= NPB:
                        ew.append(tp_tok[i - NPB])
                    ex = b.op("scalar",
                              lambda e, par=par, spar=spar, mneg=mneg,
                              lsum=lsum:
                              e.activation(P_sb[par][:], St_sb[spar][:],
                                           AF.Exp, bias=mneg,
                                           accum_out=lsum),
                              wait=ew)
                    exp_tok[i] = ex
                    rc = b.op("vector",
                              lambda e, rl=rl, lsum=lsum: e.reciprocal(
                                  rl, lsum),
                              wait=[ex])
                    nt = b.op("vector",
                              lambda e, par=par, rl=rl: e.tensor_scalar_mul(
                                  P_sb[par][:], P_sb[par][:], rl),
                              wait=[rc])
                    norm_tok[i] = nt

                if 2 <= i <= niter + 1:
                    hpp, qtpp = idx_hq(i - 2)
                    chpp = hpp // 2 if is_a else hpp
                    if is_a:
                        dest = afm[64 * (hpp % 2):64 * (hpp % 2) + 64,
                                   chpp * SEQ + qtpp * P:
                                   chpp * SEQ + (qtpp + 1) * P]
                        srcp = ps_PV[0:64, :]
                    else:
                        dest = afm[:, chpp * SEQ + qtpp * P:
                                   chpp * SEQ + (qtpp + 1) * P]
                        srcp = ps_PV[:, :]
                    pvev_tok[i - 2] = b.op(
                        "scalar",
                        lambda e, dest=dest, srcp=srcp: e.activation(
                            dest, srcp, AF.Copy),
                        wait=[pv_tok[i - 2]])

                if 1 <= i <= niter:
                    hp, qtp = idx_hq(i - 1)
                    chp = hp // 2 if is_a else hp
                    parp = (i - 1) % NPB
                    w = [tp_tok[i - 1]]
                    if (i - 1 - NPB) >= 0:
                        w.append(pv_tok[i - 1 - NPB])
                    ptev_tok[i - 1] = b.op(
                        "vector",
                        lambda e, parp=parp: e.tensor_copy(
                            PT_sb[parp][:], ps_Tb[:]),
                        wait=w)
                    ptok = None
                    for kb in range(NKB):
                        w = []
                        if kb == 0:
                            w = [ptev_tok[i - 1]]
                            if i >= 2:
                                w.append(pvev_tok[i - 2])
                        if is_a:
                            lhs = (lambda kb=kb, hp=hp:
                                   vsb[:, kb * DM + hp * HD_A:
                                       kb * DM + (hp + 1) * HD_A])
                            out = ps_PV[0:64, :]
                        else:
                            lhs = (lambda kb=kb, hp=hp:
                                   vsb[:, kb * DM + hp * HD_B:
                                       kb * DM + (hp + 1) * HD_B])
                            out = ps_PV[:, :]
                        ptok = b.op(
                            "tensor",
                            lambda e, kb=kb, lhs=lhs, out=out, parp=parp:
                            e.matmul(
                                out, lhs(),
                                PT_sb[parp][:, kb * P:(kb + 1) * P],
                                start=(kb == 0), stop=(kb == NKB - 1)),
                            wait=w, inc=f_v if kb == NKB - 1 else None)
                    pv_tok[i - 1] = ptok

                if i < niter:
                    ttok = None
                    for kb in range(NKB):
                        w = []
                        if kb == 0:
                            w = [norm_tok[i]]
                            if i >= 1:
                                w.append(ptev_tok[i - 1])
                        ttok = b.op(
                            "tensor",
                            lambda e, kb=kb, par=par: e.transpose(
                                ps_Tb[:, kb * P:(kb + 1) * P],
                                P_sb[par][:, kb * P:(kb + 1) * P],
                                identb[:]),
                            wait=w, inc=f_t if kb == NKB - 1 else None)
                    tp_tok[i] = ttok

            if self.tap == ("afm" + variant, s):
                self._tapd(afm, 8 * SEQ, [pvev_tok[niter - 1]])
                return None
            return pvev_tok[niter - 1]

def kernel(**inputs):
    ins = _prepare_inputs(inputs)
    prog = Prog()
    nc = prog.build()
    in_maps = []
    for c in range(8):
        m = dict(ins["common"])
        m["x"] = ins["x_shards"][c]
        in_maps.append(m)
    res = run_bass_kernel_spmd(nc, in_maps, core_ids=list(range(8)))
    out = np.concatenate(
        [np.asarray(r["y"]).reshape(NSEQ, SEQ, DM) for r in res.results], axis=0)
    return out.astype(np.float32)


def _prepare_inputs(d):
    x = np.asarray(d["L0_attn_a_hidden_states"], np.float32)
    x_shards = [np.ascontiguousarray(x[2 * c:2 * c + 2].reshape(TOK, DM))
                for c in range(8)]

    def T(a):
        return np.ascontiguousarray(np.asarray(a, np.float32).T)

    def lay_s(s):
        return np.ascontiguousarray(
            np.asarray(s, np.float32).reshape(NOC_DH, P).T)

    common = {
        "waqT": T(d["L0_attn_a_Wq"]),
        "wakT": T(d["L0_attn_a_Wk"]),
        "wavT": T(d["L0_attn_a_Wv"]),
        "waoT": T(d["L0_attn_a_Wo"]),
        "wbq": np.ascontiguousarray(np.asarray(d["L0_attn_b_w_qs"], np.float32)),
        "wbk": np.ascontiguousarray(np.asarray(d["L0_attn_b_w_ks"], np.float32)),
        "wbv": np.ascontiguousarray(np.asarray(d["L0_attn_b_w_vs"], np.float32)),
        "wbo": np.ascontiguousarray(np.asarray(d["L0_attn_b_w_out"], np.float32)),
        "wbrel": np.ascontiguousarray(
            np.asarray(d["L0_attn_b_w_rel_pos_k"], np.float32)),
        "wbreb": np.ascontiguousarray(
            np.asarray(d["L0_attn_b_rel_embed_bias"], np.float32).reshape(
                NH_B, HD_B)),
        "wbrpb": np.ascontiguousarray(
            np.asarray(d["L0_attn_b_rel_pos_bias"], np.float32).reshape(
                NH_B, HD_B)),
        "boutrow": np.ascontiguousarray(
            np.asarray(d["L0_attn_b_b_out"], np.float32).reshape(1, DM)),
        "posT": _positions_T(),
        "w1T": T(d["L0_ffn_w1_w"]),
        "w2T": T(d["L0_ffn_w2_w"]),
        "h1T": T(d["L0_ffn_h1_w"]),
        "h2T": T(d["L0_ffn_h2_w"]),
        "w3T": T(d["L0_ffn_w3_w"]),
        "s1l": lay_s(d["L0_ffn_w1_s"]),
        "s2l": lay_s(d["L0_ffn_w2_s"]),
        "hs1l": lay_s(d["L0_ffn_h1_s"]),
        "hs2l": lay_s(d["L0_ffn_h2_s"]),
        "s3row": np.ascontiguousarray(
            np.asarray(d["L0_ffn_w3_s"], np.float32).reshape(1, DM)),
    }
    return {"x_shards": x_shards, "common": common}
